# version 07
# baseline (speedup 1.0000x reference)
import numpy as np
import ml_dtypes
from contextlib import ExitStack

import concourse.bass as bass
import concourse.mybir as mybir
from concourse.bass_utils import run_bass_kernel_spmd


def _register_fq_ops():
    from concourse import dve_ops as D
    from concourse.dve_op import DveOp
    from concourse.dve_spec import (Spec, Src0, Src1, C0, C1, C2, lower,
                                    maxx, AluOp, Bin)

    have = {o.name: o for o in D.OPS}
    if "RNE_FQS" in have:
        return tuple(have[n] for n in ("DIV127G", "RNE_FQS", "RECIP_PS", "RNE_FQT"))

    donors = iter(("DVEHASH32L_SEED", "DVEHASH32L_BODY", "DVEHASH32L_DRAIN",
                   "TENSOR_MASK_FP8"))

    def _mk(name, body, ref):
        spec = Spec(body=body)
        sha = lower(spec, ver="v3", perf_en=False, name=name).sha("v3")
        op = DveOp(name, spec, subdim=False, uops_sha={"v3": sha}, reference=ref)
        D.OPS.append(op)
        D.CUSTOM_DVE_SPECS[name] = spec
        D.CUSTOM_DVE_REFS[name] = op.ref
        D._SUB_OPCODE_FOR_NAME[name] = D._SUB_OPCODE_FOR_NAME[next(donors)]
        return op

    def _f64(a):
        return a.dtype == np.float64

    div127 = _mk("DIV127G", maxx(Src0, C1) * C0,
                 lambda in0, in1, s0, s1, imm2:
                 np.maximum(in0, s1) / 127.0 if _f64(in0)
                 else np.maximum(in0, np.float32(s1)) * np.float32(s0))
    fqs = _mk("RNE_FQS", ((Src0 * C1 + C2) - C2) * C0,
              lambda in0, in1, s0, s1, imm2:
              np.round(in0 * (1.0 / s0)) * s0 if _f64(in0)
              else np.round(in0 * s1) * s0)
    notx = Bin(AluOp.BITWISE_NOT, Src0, Src0)
    y0 = notx * C0
    y1 = y0 * (C1 - Src0 * y0)

    def _ref_recip_pass(in0, in1, s0, s1, imm2):
        if _f64(in0):
            return in0.copy()
        not_x = (~in0.view(np.int32)).view(np.float32)
        y0 = not_x * np.float32(s0)
        y1 = y0 * (np.float32(s1) - in0 * y0)
        return y1 * (np.float32(imm2) - in0 * y1)

    rp = _mk("RECIP_PS", y1 * (C2 - Src0 * y1), _ref_recip_pass)
    fqt = _mk("RNE_FQT", (Src0 * Src1 + C2) - C2,
              lambda in0, in1, s0, s1, imm2:
              np.round(in0 * (1.0 / in1)) if _f64(in0)
              else np.round(in0 * in1))
    return div127, fqs, rp, fqt


RECIP_CONSTS = dict(s0=-0.23549792, s1=2.0017324, imm2=2.0)
DIV127G, RNE_FQS, RECIP_PS, RNE_FQT = _register_fq_ops()

F32 = mybir.dt.float32
BF16 = mybir.dt.bfloat16
AF = mybir.ActivationFunctionType
ALU = mybir.AluOpType
AX = mybir.AxisListType

NCORES = 8
T = 2048
NTS = 16
G = 4
E = 2048
KE = 16
C = 4096
NCC = 32
HD = 128
F = 2048
NB = 256
M_ROUND = 12582912.0
QMAX = 127.0


def _flt(deps):
    return tuple(d for d in deps if d is not None)


NRING = 8


class Sched:

    ENGINES = ("SP", "GP", "PE", "DVE", "ACT")

    def __init__(self, nc):
        self.nc = nc
        self.ops = {e: [] for e in self.ENGINES}
        self.count = {}
        self.dma_i = {"SP": 0, "GP": 0}
        self.sems = {}
        self.semnames = []
        for e in ("PE", "DVE", "ACT", "PL"):
            self.semnames.append(e)
            self.count[e] = 0
        for q in ("SP", "GP"):
            for r in range(NRING):
                n = f"{q}{r}"
                self.semnames.append(n)
                self.count[n] = 0

    def alloc_sems(self, es):
        for n in self.semnames:
            self.sems[n] = es.enter_context(self.nc.semaphore(f"s_{n}"))

    def op(self, eng, fn, deps=(), inc=False):
        deps = list(_flt(deps))
        if eng == "PLC":
            self.count["PL"] += 1
            tok = ("PL", self.count["PL"])
            self.ops["GP"].append((tuple(deps), fn, tok))
            return tok
        if eng in ("DVE", "ACT"):
            inc = True
        tok = None
        if eng in ("SP", "GP"):
            i = self.dma_i[eng]
            self.dma_i[eng] = i + 1
            sem = f"{eng}{i % NRING}"
            if self.count[sem] > 0:
                deps.append((sem, self.count[sem]))
            self.count[sem] += 16
            tok = (sem, self.count[sem])
        elif inc:
            self.count[eng] += 1
            tok = (eng, self.count[eng])
        self.ops[eng].append((tuple(deps), fn, tok))
        return tok

    def queue_toks(self, q):
        return [(f"{q}{r}", self.count[f"{q}{r}"]) for r in range(NRING)]

    def barrier(self, real_ops):
        toks = [("PL", self.count["PL"])]
        for e in ("PE", "DVE", "ACT"):
            toks.append(self.op(e, real_ops[e], inc=True))
        toks += self.queue_toks("SP") + self.queue_toks("GP")
        for e in self.ENGINES:
            self.ops[e].append((tuple(toks), None, None))
        return toks

    def replay(self, eng_name, engine):
        waited = {}
        self_serial = eng_name in ("DVE", "ACT")
        prev_count = 0
        prev_plc = 0
        for deps, fn, tok in self.ops[eng_name]:
            for (sname, val) in deps:
                if sname == eng_name or val <= 0:
                    continue
                if waited.get(sname, 0) < val:
                    engine.wait_ge(self.sems[sname], val)
                    waited[sname] = val
            if fn is None:
                continue
            is_plc = tok is not None and tok[0] == "PL"
            if self_serial and prev_count > 0:
                engine.wait_ge(self.sems[eng_name], prev_count)
            if is_plc and prev_plc > 0 and waited.get("PL", 0) < prev_plc:
                engine.wait_ge(self.sems["PL"], prev_plc)
                waited["PL"] = max(waited.get("PL", 0), prev_plc)
            inst = fn(engine)
            if tok is not None:
                step = 16 if tok[0] not in ("PE", "DVE", "ACT", "PL") else 1
                inst.then_inc(self.sems[tok[0]], step)
                if tok[0] == eng_name:
                    prev_count = tok[1]
                if is_plc:
                    prev_plc = tok[1]


def build_program(phases="ABC", bsel="wasqm", alvl=9):
    nc = _build_program(phases, bsel, alvl)
    mybir.codegen_inst_isa_subclasses(nc)
    return nc


def _build_program(phases="ABC", bsel="wasqm", alvl=9):
    _alvl_n = alvl
    nc = bass.Bass("TRN2", target_bir_lowering=False)

    x_d = nc.dram_tensor("x", [T, E], F32, kind="ExternalInput").ap()
    xb_d = nc.dram_tensor("xb", [NB, E], F32, kind="ExternalInput").ap()
    wq1_d = nc.dram_tensor("wq1", [E, HD], F32, kind="ExternalInput").ap()
    wk1_d = nc.dram_tensor("wk1", [E, HD], F32, kind="ExternalInput").ap()
    wq2_d = nc.dram_tensor("wq2", [HD, C], F32, kind="ExternalInput").ap()
    wk2_d = nc.dram_tensor("wk2", [HD, C], F32, kind="ExternalInput").ap()
    wv_d = nc.dram_tensor("wv", [E, C], F32, kind="ExternalInput").ap()
    wo_d = nc.dram_tensor("wo", [C, E], F32, kind="ExternalInput").ap()
    kern_d = nc.dram_tensor("kern", [E, F], F32, kind="ExternalInput").ap()
    bias_d = nc.dram_tensor("biasr", [1, F], F32, kind="ExternalInput").ap()
    nw_d = nc.dram_tensor("nwr", [NCC, 128], F32, kind="ExternalInput").ap()
    identb_d = nc.dram_tensor("identb", [128, 128], BF16, kind="ExternalInput").ap()
    onesb_d = nc.dram_tensor("onesb", [128, 128], BF16, kind="ExternalInput").ap()
    y_d = nc.dram_tensor("y", [T, E], F32, kind="ExternalOutput").ap()

    gw_sc = nc.dram_tensor("gw_sc", [G, NCC, 128, 512], BF16)
    x2_sc = nc.dram_tensor("x2_sc", [NTS, 128, E], F32)
    kq_sc = nc.dram_tensor("kq_sc", [KE, 128, F], BF16)
    ssq_sc = nc.dram_tensor("ssq_sc", [G, 512], F32)
    amx_sc = nc.dram_tensor("amx_sc", [2, 16, 128], F32)

    sch = Sched(nc)
    es = ExitStack()
    sch.alloc_sems(es)

    IDT = es.enter_context(nc.sbuf_tensor("IDT", [128, 128], BF16))
    ONE = es.enter_context(nc.sbuf_tensor("ONE", [128, 128], BF16))
    SSQTM = es.enter_context(nc.sbuf_tensor("SSQTM", [128, NTS], F32))
    CONSTS = es.enter_context(nc.sbuf_tensor("CONSTS", [128, 6], F32))

    with nc.Block() as block, es:
        esA = ExitStack()
        XNT = esA.enter_context(nc.sbuf_tensor("XNT", [128, KE * T], BF16))
        XNTB = esA.enter_context(nc.sbuf_tensor("XNTB", [128, KE * NB], BF16))
        WVS = [esA.enter_context(nc.sbuf_tensor(f"WVS{i}", [128, KE * 128], BF16)) for i in range(2)]
        WK2 = esA.enter_context(nc.sbuf_tensor("WK2", [128, C], BF16))
        WQ2 = esA.enter_context(nc.sbuf_tensor("WQ2", [128, C], BF16))
        QPT = esA.enter_context(nc.sbuf_tensor("QPT", [128, T], BF16))
        LPT = esA.enter_context(nc.sbuf_tensor("LPT", [128, T], BF16))
        LPTB = esA.enter_context(nc.sbuf_tensor("LPTB", [128, NB], BF16))
        NWC = esA.enter_context(nc.sbuf_tensor("NWC", [128, NCC], F32))
        H0C = esA.enter_context(nc.sbuf_tensor("H0C", [128, NCC], F32))
        SSQR = esA.enter_context(nc.sbuf_tensor("SSQR", [1, 512], F32))

        sch.op("DVE", lambda e: e.memset(CONSTS[:, 0:1], 1e-5))
        sch.op("DVE", lambda e: e.memset(CONSTS[:, 1:2], 1e-6))
        tok_consts = sch.op("DVE", lambda e: e.memset(CONSTS[:, 2:3], 1.0), inc=True)
        tok_idt = sch.op("SP", lambda e: e.dma_start(out=IDT[:], in_=identb_d), inc=True)
        tok_one = sch.op("SP", lambda e: e.dma_start(out=ONE[:], in_=onesb_d), inc=True)
        tok_nwc = sch.op("SP", lambda e: e.dma_start(
            out=NWC[:], in_=nw_d.rearrange("cc p -> p cc"),
            allow_slow_non_contiguous=True), inc=True)
        tok_wk2 = sch.op("GP", lambda e: e.dma_start(out=WK2[:], in_=wk2_d), inc=True)
        tok_wq2 = sch.op("GP", lambda e: e.dma_start(out=WQ2[:], in_=wq2_d), inc=True)

        if "A" in phases:
            esP = ExitStack()
            XP = [esP.enter_context(nc.sbuf_tensor(f"XP{i}", [128, E], F32)) for i in range(2)]
            WQ1 = esP.enter_context(nc.sbuf_tensor("WQ1", [128, KE * HD], BF16))
            WK1 = esP.enter_context(nc.sbuf_tensor("WK1", [128, KE * HD], BF16))
            XNB = [esP.enter_context(nc.sbuf_tensor(f"XNB{i}", [128, E], BF16)) for i in range(2)]
            PSS = esP.enter_context(nc.sbuf_tensor("PSS", [128, NTS + 2], F32))
            NMS = esP.enter_context(nc.sbuf_tensor("NMS", [128, NTS + 2], F32))
            RSS = esP.enter_context(nc.sbuf_tensor("RSS", [128, NTS + 2], F32))
            TV1 = esP.enter_context(nc.sbuf_tensor("TV1", [128, 1], F32))
            TV2 = esP.enter_context(nc.sbuf_tensor("TV2", [128, 1], F32))

            tok_wq1 = sch.op("GP", lambda e: e.dma_start(
                out=WQ1[:].rearrange("p (ke h) -> p ke h", ke=KE),
                in_=wq1_d.rearrange("(ke p) h -> p ke h", p=128)), inc=True)
            tok_wk1 = sch.op("GP", lambda e: e.dma_start(
                out=WK1[:].rearrange("p (ke h) -> p ke h", ke=KE),
                in_=wk1_d.rearrange("(ke p) h -> p ke h", p=128)), inc=True)

            def load_wvs(cc, deps):
                return sch.op("GP", lambda e, cc=cc: e.dma_start(
                    out=WVS[cc % 2][:].rearrange("p (ke q) -> p ke q", ke=KE),
                    in_=wv_d.rearrange("(ke p) c -> p ke c", p=128)[:, :, cc * 128:(cc + 1) * 128]),
                    deps=deps, inc=True)

            tok_wvs = {}
            tok_wvs[0] = load_wvs(0, [])
            tok_wvs[1] = load_wvs(1, [])

            esPP = ExitStack()
            TRP = [esPP.enter_context(nc.psum_tensor(f"TRP{i}", [128, 1024], BF16)) for i in range(2)]
            QPP = esPP.enter_context(nc.psum_tensor("QPP", [128, 128], F32))
            LPP = esPP.enter_context(nc.psum_tensor("LPP", [128, 128], F32))

            last_xp_reader = [None, None]
            prev_ev = [None, None]
            prev_tp_last = [None, None]
            prev_qev = None
            prev_lev = None
            prev_ln_tok = None
            last_bnd_ev = []
            tok_lptb_done = None
            stat_toks = {}

            def p_stats(ts):
                nonlocal prev_ln_tok
                r = ts % 2
                bnd = ts >= NTS
                src_ = (x_d[ts * 128:(ts + 1) * 128, :] if not bnd
                        else xb_d[(ts - NTS) * 128:(ts - NTS + 1) * 128, :])
                tok_x = sch.op("SP", lambda e, src_=src_, r=r: e.dma_start(
                    out=XP[r][:], in_=src_), deps=[last_xp_reader[r]], inc=True)
                tok_sq = sch.op("ACT", lambda e, r=r, ts=ts: e.activation(
                    XNB[r][:], XP[r][:], AF.Square, accum_out=PSS[:, ts:ts + 1]),
                    deps=[tok_x, prev_tp_last[r]], inc=True)
                sch.op("DVE", lambda e, r=r, ts=ts: e.tensor_reduce(
                    NMS[:, ts:ts + 1], XP[r][:], AX.X, ALU.add, negate=True),
                    deps=[tok_x, prev_ln_tok])
                sch.op("DVE", lambda e, ts=ts: e.tensor_scalar(
                    NMS[:, ts:ts + 1], NMS[:, ts:ts + 1], 1.0 / E, None, ALU.mult))
                sch.op("DVE", lambda e, ts=ts: e.tensor_tensor(
                    TV1[:], NMS[:, ts:ts + 1], NMS[:, ts:ts + 1], ALU.mult))
                sch.op("DVE", lambda e, ts=ts: e.tensor_scalar(
                    TV2[:], PSS[:, ts:ts + 1], 1.0 / E, None, ALU.mult), deps=[tok_sq])
                tok_v2 = sch.op("DVE", lambda e: e.tensor_tensor(
                    TV2[:], TV2[:], TV1[:], ALU.subtract), inc=True)
                prev_ln_tok = sch.op("ACT", lambda e, ts=ts: e.activation(
                    RSS[:, ts:ts + 1], TV2[:], AF.Ln, bias=CONSTS[:, 0:1]),
                    deps=[tok_v2, tok_consts], inc=True)
                tok_rs = sch.op("ACT", lambda e, ts=ts: e.activation(
                    RSS[:, ts:ts + 1], RSS[:, ts:ts + 1], AF.Exp, scale=-0.5), inc=True)
                stat_toks[ts] = (tok_sq, tok_rs)

            def p_finish(ts):
                nonlocal prev_qev, prev_lev, tok_lptb_done
                r = ts % 2
                bnd = ts >= NTS
                tok_sq, tok_rs = stat_toks[ts]
                tok_xn = sch.op("DVE", lambda e, r=r, ts=ts: e.tensor_scalar(
                    XNB[r][:], XP[r][:], NMS[:, ts:ts + 1], RSS[:, ts:ts + 1],
                    ALU.add, ALU.mult), deps=[tok_sq, tok_rs], inc=True)
                last_xp_reader[r] = tok_xn
                tok_tp = [None, None]
                for half in range(2):
                    for j in range(8):
                        ke = half * 8 + j
                        t = sch.op("PE", lambda e, ke=ke, half=half, j=j, r=r: e.transpose(
                            TRP[half][:, j * 128:(j + 1) * 128],
                            XNB[r][:, ke * 128:(ke + 1) * 128], IDT[:]),
                            deps=([tok_xn, tok_idt, prev_ev[half]] if j == 0 else []),
                            inc=(j == 7))
                        if j == 7:
                            tok_tp[half] = t
                prev_tp_last[r] = tok_tp[1]
                evs = []
                for half in range(2):
                    if not bnd:
                        dst = XNT[:].rearrange("p (ke t) -> p ke t", ke=KE)[
                            :, half * 8:(half + 1) * 8, ts * 128:(ts + 1) * 128]
                    else:
                        dst = XNTB[:].rearrange("p (ke t) -> p ke t", ke=KE)[
                            :, half * 8:(half + 1) * 8, (ts - NTS) * 128:(ts - NTS + 1) * 128]
                    ev = sch.op("DVE", lambda e, dst=dst, half=half: e.tensor_copy(
                        dst, TRP[half][:].rearrange("p (j t) -> p j t", j=8)),
                        deps=[tok_tp[half]], inc=True)
                    prev_ev[half] = ev
                    evs.append(ev)
                if bnd:
                    last_bnd_ev.extend(evs)
                if not bnd:
                    tq_mm = None
                    for ke in range(KE):
                        tq_mm = sch.op("PE", lambda e, ke=ke, ts=ts: e.matmul(
                            QPP[:], WQ1[:, ke * 128:(ke + 1) * 128],
                            XNT[:, ke * T + ts * 128: ke * T + (ts + 1) * 128],
                            start=(ke == 0), stop=(ke == KE - 1)),
                            deps=(evs + [tok_wq1, prev_qev] if ke == 0 else []),
                            inc=(ke == KE - 1))
                    tl_mm = None
                    for ke in range(KE):
                        tl_mm = sch.op("PE", lambda e, ke=ke, ts=ts: e.matmul(
                            LPP[:], WK1[:, ke * 128:(ke + 1) * 128],
                            XNT[:, ke * T + ts * 128: ke * T + (ts + 1) * 128],
                            start=(ke == 0), stop=(ke == KE - 1)),
                            deps=([tok_wk1, prev_lev] if ke == 0 else []),
                            inc=(ke == KE - 1))
                    prev_qev = sch.op("DVE", lambda e, ts=ts: e.tensor_copy(
                        QPT[:, ts * 128:(ts + 1) * 128], QPP[:]), deps=[tq_mm], inc=True)
                    prev_lev = sch.op("DVE", lambda e, ts=ts: e.tensor_copy(
                        LPT[:, ts * 128:(ts + 1) * 128], LPP[:]), deps=[tl_mm], inc=True)
                else:
                    tb_ = ts - NTS
                    tl_mm = None
                    for ke in range(KE):
                        tl_mm = sch.op("PE", lambda e, ke=ke, tb_=tb_: e.matmul(
                            LPP[:], WK1[:, ke * 128:(ke + 1) * 128],
                            XNTB[:, ke * NB + tb_ * 128: ke * NB + (tb_ + 1) * 128],
                            start=(ke == 0), stop=(ke == KE - 1)),
                            deps=(evs + [prev_lev] if ke == 0 else []),
                            inc=(ke == KE - 1))
                    prev_lev = tok_lptb_done = sch.op("DVE", lambda e, tb_=tb_: e.tensor_copy(
                        LPTB[:, tb_ * 128:(tb_ + 1) * 128], LPP[:]), deps=[tl_mm], inc=True)

            for k in range(NTS + 3):
                if k < NTS + 2:
                    p_stats(k)
                if k >= 1:
                    p_finish(k - 1)

            esPP.close()
            esP.close()

            esM = ExitStack()
            SCF = [esM.enter_context(nc.sbuf_tensor(f"SCF{i}", [128, 512], F32)) for i in range(2)]
            SCA = [esM.enter_context(nc.sbuf_tensor(f"SCA{i}", [128, 512], F32)) for i in range(2)]
            SCKV = [esM.enter_context(nc.sbuf_tensor(f"SCKV{i}", [128, 512], F32)) for i in range(2)]
            SCH_ = [esM.enter_context(nc.sbuf_tensor(f"SCH{i}", [128, 512], F32)) for i in range(2)]
            SCSQ = [esM.enter_context(nc.sbuf_tensor(f"SCSQ{i}", [128, 512], F32)) for i in range(2)]
            SCQ = [esM.enter_context(nc.sbuf_tensor(f"SCQ{i}", [128, 512], F32)) for i in range(2)]
            SCG = [esM.enter_context(nc.sbuf_tensor(f"SCG{i}", [128, 512], F32)) for i in range(2)]
            SCGQ = [esM.enter_context(nc.sbuf_tensor(f"SCGQ{i}", [128, 512], BF16)) for i in range(2)]
            GWS = [esM.enter_context(nc.sbuf_tensor(f"GWS{i}", [128, 512], BF16)) for i in range(2)]
            AFt = esM.enter_context(nc.sbuf_tensor("AFt", [128, NB], F32))
            AAt = esM.enter_context(nc.sbuf_tensor("AAt", [128, NB], F32))
            AKV = esM.enter_context(nc.sbuf_tensor("AKV", [128, NB], F32))
            AH = esM.enter_context(nc.sbuf_tensor("AH", [128, NB], F32))

            esMP = ExitStack()
            VP = [esMP.enter_context(nc.psum_tensor(f"VP{i}", [128, 512], F32)) for i in range(2)]
            LQP = esMP.enter_context(nc.psum_tensor("LQP", [128, 1024], F32))
            SQG = [esMP.enter_context(nc.psum_tensor(f"SQG{i}", [128, 512], F32)) for i in range(G)]

            tok_kv = {}
            tok_tg = {}
            tok_f = {}
            tok_qmul = {}
            tok_gsq = {}
            tok_gwdma = {}
            tok_ssqmm = {}
            tok_scan = {}
            tok_kvb = {}

            for cc in range(NCC):
                if cc + 2 < NCC:
                    pass
                tvb = None
                dep_a0 = [tok_wvs[cc], tok_kv.get(cc * G - 2)]
                if cc == 0:
                    dep_a0 += last_bnd_ev
                for ke in range(KE):
                    tvb = sch.op("PE", lambda e, ke=ke, cc=cc: e.matmul(
                        VP[0][:, 0:NB], WVS[cc % 2][:, ke * 128:(ke + 1) * 128],
                        XNTB[:, ke * NB:(ke + 1) * NB],
                        start=(ke == 0), stop=(ke == KE - 1)),
                        deps=(dep_a0 if ke == 0 else []), inc=(ke == KE - 1))
                tok_lfb = sch.op("PE", lambda e, cc=cc: e.matmul(
                    LQP[:, 0:NB], WK2[:, cc * 128:(cc + 1) * 128], LPTB[:],
                    start=True, stop=True),
                    deps=[tok_wk2, tok_lptb_done, tok_f.get(cc * G - 1)], inc=True)
                tok_fb = sch.op("ACT", lambda e: e.activation(AFt[:], LQP[:, 0:NB], AF.Sigmoid),
                                deps=[tok_lfb, tok_kvb.get(cc - 1)], inc=True)
                sch.op("ACT", lambda e: e.activation(AAt[:], AFt[:], AF.Sigmoid),
                       deps=[("DVE", 0)])
                tok_ab = sch.op("ACT", lambda e: e.activation(
                    AFt[:], AAt[:], AF.Copy, bias=1.0, scale=-1.0), inc=True)
                tkvb = sch.op("DVE", lambda e: e.tensor_tensor(
                    AKV[:], AFt[:], VP[0][:, 0:NB], ALU.mult), deps=[tok_ab, tvb], inc=True)
                tok_kvb[cc] = tkvb
                sch.op("DVE", lambda e: e.tensor_tensor_scan(
                    AH[:], AAt[:], AKV[:], 0.0, ALU.mult, ALU.add))
                sch.op("DVE", lambda e, cc=cc: e.tensor_copy(
                    H0C[:, cc:cc + 1], AH[:, NB - 1:NB]))
                for g in range(G):
                    i = cc * G + g
                    r = g % 2
                    if g == 0:
                        depv = [tkvb]
                    elif g == 1:
                        depv = [tok_kv.get(i - 2 - 4 + 4) or tok_kv.get((cc - 1) * G + 3)]
                        depv = [tok_kv.get((cc - 1) * G + 3)]
                    else:
                        depv = [tok_kv.get(i - 2)]
                    tok_v = None
                    for ke in range(KE):
                        tok_v = sch.op("PE", lambda e, ke=ke, cc=cc, g=g, r=r: e.matmul(
                            VP[r][:], WVS[cc % 2][:, ke * 128:(ke + 1) * 128],
                            XNT[:, ke * T + g * 512: ke * T + (g + 1) * 512],
                            start=(ke == 0), stop=(ke == KE - 1)),
                            deps=(depv if ke == 0 else []), inc=(ke == KE - 1))
                    if g == G - 1:
                        if cc + 2 < NCC:
                            tok_wvs[cc + 2] = load_wvs(cc + 2, [tok_v])
                    tok_lf = sch.op("PE", lambda e, cc=cc, g=g: e.matmul(
                        LQP[:, 0:512], WK2[:, cc * 128:(cc + 1) * 128],
                        LPT[:, g * 512:(g + 1) * 512], start=True, stop=True),
                        deps=[tok_f.get(i - 1), (tok_fb if g == 0 else None)], inc=True)
                    tok_q2 = sch.op("PE", lambda e, cc=cc, g=g: e.matmul(
                        LQP[:, 512:1024], WQ2[:, cc * 128:(cc + 1) * 128],
                        QPT[:, g * 512:(g + 1) * 512], start=True, stop=True),
                        deps=[tok_qmul.get(i - 1), tok_wq2], inc=True)
                    if i > 1:
                        gp = (i - 2) % G
                        ccp = (i - 2) // G
                        tok_ssqmm[i - 2] = sch.op("PE", lambda e, gp=gp, ccp=ccp: e.matmul(
                            SQG[gp][:], ONE[:], SCGQ[gp % 2][:],
                            start=(ccp == 0), stop=(ccp == NCC - 1)),
                            deps=[tok_gsq[i - 2], tok_one], inc=True)
                    tf = sch.op("ACT", lambda e, r=r: e.activation(
                        SCF[r][:], LQP[:, 0:512], AF.Sigmoid),
                        deps=[tok_lf, tok_kv.get(i - 2)], inc=True)
                    tok_f[i] = tf
                    sch.op("ACT", lambda e, r=r: e.activation(SCA[r][:], SCF[r][:], AF.Sigmoid),
                           deps=[tok_scan.get(i - 2)])
                    tok_a = sch.op("ACT", lambda e, r=r: e.activation(
                        SCF[r][:], SCA[r][:], AF.Copy, bias=1.0, scale=-1.0), inc=True)
                    tok_sq = sch.op("ACT", lambda e, r=r: e.activation(
                        SCSQ[r][:], LQP[:, 512:1024], AF.Sigmoid), deps=[tok_q2], inc=True)
                    tkv = sch.op("DVE", lambda e, r=r: e.tensor_tensor(
                        SCKV[r][:], SCF[r][:], VP[r][:], ALU.mult),
                        deps=[tok_a, tok_v], inc=True)
                    tok_kv[i] = tkv
                    init = H0C[:, cc:cc + 1] if g == 0 else SCH_[(g - 1) % 2][:, 511:512]
                    tok_scan[i] = sch.op(
                        "DVE", lambda e, r=r, init=init: e.tensor_tensor_scan(
                            SCH_[r][:], SCA[r][:], SCKV[r][:], init, ALU.mult, ALU.add),
                        inc=True)
                    tq = sch.op("DVE", lambda e, r=r: e.tensor_tensor(
                        SCQ[r][:], SCSQ[r][:], LQP[:, 512:1024], ALU.mult),
                        deps=[tok_sq], inc=True)
                    tok_qmul[i] = tq
                    tg = sch.op("DVE", lambda e, r=r: e.tensor_tensor(
                        SCG[r][:], SCQ[r][:], SCH_[r][:], ALU.mult),
                        deps=[tok_gsq.get(i - 2)], inc=True)
                    tok_tg[i] = tg
                    tgw = sch.op("DVE", lambda e, r=r, cc=cc: e.tensor_scalar(
                        GWS[r][:], SCG[r][:], NWC[:, cc:cc + 1], None, ALU.mult),
                        deps=[tok_gwdma.get(i - 2), tok_nwc], inc=True)
                    if i > 0:
                        rp = ((i - 1) % G) % 2
                        tok_gsq[i - 1] = sch.op("ACT", lambda e, rp=rp: e.activation(
                            SCGQ[rp][:], SCG[rp][:], AF.Square),
                            deps=[tok_tg[i - 1], tok_ssqmm.get(i - 3)], inc=True)
                    tok_gwdma[i] = sch.op("SP", lambda e, g=g, cc=cc, r=r: e.dma_start(
                        out=gw_sc.ap()[g, cc], in_=GWS[r][:]), deps=[tgw], inc=True)
            i_last = NCC * G - 1
            rl = (i_last % G) % 2
            tok_gsq[i_last] = sch.op("ACT", lambda e, rl=rl: e.activation(
                SCGQ[rl][:], SCG[rl][:], AF.Square),
                deps=[tok_tg[i_last], tok_ssqmm.get(i_last - 2)], inc=True)
            gpp = (i_last - 1) % G
            tok_ssqmm[i_last - 1] = sch.op("PE", lambda e, gpp=gpp: e.matmul(
                SQG[gpp][:], ONE[:], SCGQ[gpp % 2][:], start=False, stop=True),
                deps=[tok_gsq[i_last - 1], tok_one], inc=True)
            tok_ssq_tail = sch.op("PE", lambda e: e.matmul(
                SQG[G - 1][:], ONE[:], SCGQ[(G - 1) % 2][:], start=(NCC == 1), stop=True),
                deps=[tok_gsq[i_last], tok_one], inc=True)
        sch.barrier(real_ops={
            "PE": lambda en: en.ldweights(IDT[:]),
            "DVE": lambda en: en.tensor_copy(CONSTS[:, 3:4], CONSTS[:, 0:1]),
            "ACT": lambda en: en.activation(CONSTS[:, 4:5], CONSTS[:, 0:1], AF.Copy),
        })
        tok_ssq_done = None
        if "A" in phases:
            prev_row_dma = None
            for g in range(G):
                te = sch.op("DVE", lambda e, g=g: e.tensor_copy(SSQR[:], SQG[g][0:1, :]),
                            deps=[prev_row_dma, tok_ssq_tail], inc=True)
                prev_row_dma = sch.op("SP", lambda e, g=g: e.dma_start(
                    out=ssq_sc.ap()[g:g + 1, :], in_=SSQR[:]), deps=[te], inc=True)
                tok_ssq_done = te
            tok_ssq_dma = prev_row_dma
            tok_ssq_q = sch.queue_toks("SP")
            esMP.close()
            esM.close()
        esA.close()

        if "B" not in phases:
            sch.op("SP", lambda e: e.wait_ge(sch.sems["SP"], sch.count["SP"]))
            _replay_all(sch, block)
            return nc
        esB = ExitStack()
        WO = esB.enter_context(nc.sbuf_tensor("WO", [128, NCC * E], BF16))
        GWG = esB.enter_context(nc.sbuf_tensor("GWG", [128, NCC * 512], BF16))
        RSTD = esB.enter_context(nc.sbuf_tensor("RSTD", [128, NTS], F32))
        BST = esB.enter_context(nc.sbuf_tensor("BST", [128, 8], F32))
        ATN = esB.enter_context(nc.sbuf_tensor("ATN", [128, E], F32))
        XT = esB.enter_context(nc.sbuf_tensor("XT", [128, E], F32))
        KCH = esB.enter_context(nc.sbuf_tensor("KCH", [128, F], F32))
        KCB = esB.enter_context(nc.sbuf_tensor("KCB", [128, F], BF16))
        AMX = esB.enter_context(nc.sbuf_tensor("AMX", [128, 16], F32))
        AMP = esB.enter_context(nc.sbuf_tensor("AMP", [128, 16], F32))
        SCLF = esB.enter_context(nc.sbuf_tensor("SCLF", [128, 16], F32))
        IRP = esB.enter_context(nc.sbuf_tensor("IRP", [128, F], F32))
        SRP = esB.enter_context(nc.sbuf_tensor("SRP", [128, F], F32))
        SQJ = esB.enter_context(nc.sbuf_tensor("SQJ", [128, E // 2], BF16))

        esBP = ExitStack()
        TRP2 = [esBP.enter_context(nc.psum_tensor(f"TRP2{i}", [128, 1024], BF16)) for i in range(2)]
        OP = [esBP.enter_context(nc.psum_tensor(f"OP{i}", [128, 1024], F32)) for i in range(3)]

        tok_wo = []
        if "w" in bsel:
            for cc in range(NCC):
                sch.op("GP", lambda e, cc=cc: e.dma_start(
                    out=WO[:, cc * E:(cc + 1) * E],
                    in_=wo_d.rearrange("(cc p) e -> p cc e", p=128)[:, cc, :]),
                    deps=(tok_ssq_q if (cc == 0 and "A" in phases) else []), inc=True)
            tok_wo = sch.queue_toks("GP")

        tok_ssqtm = sch.op("SP", lambda e: e.dma_start(
            out=SSQTM[:], in_=ssq_sc.ap().rearrange("g (tl p) -> p (g tl)", p=128),
            allow_slow_non_contiguous=True),
            deps=[tok_ssq_dma if "A" in phases else None], inc=True)
        tok_ms = sch.op("DVE", lambda e: e.tensor_scalar(
            RSTD[:], SSQTM[:], 1.0 / C, None, ALU.mult), deps=[tok_ssqtm], inc=True)
        sch.op("ACT", lambda e: e.activation(RSTD[:], RSTD[:], AF.Ln, bias=CONSTS[:, 1:2]),
               deps=[tok_ms])
        tok_rstd = sch.op("ACT", lambda e: e.activation(
            RSTD[:], RSTD[:], AF.Exp, scale=-0.5), inc=True)

        state = dict(prev_kcb=None, prev_red=[None, None], last_tp=None,
                     amx_fin=None, irp=None, srp=None, prev_kq=None,
                     prev_quant_kcb=None, last_amax_kcb=None, ev_op2=None,
                     kch_user=None, kch_junk=None)

        def emit_kq_amax(kd):
            tl = sch.op("SP", lambda e, kd=kd: e.dma_start(
                out=KCH[:], in_=kern_d.rearrange("(kd p) f -> p kd f", p=128)[:, kd, :]),
                deps=[state["prev_kcb"]], inc=True)
            tkc = sch.op("DVE", lambda e: e.tensor_copy(KCB[:], KCH[:]),
                         deps=[tl, state["last_tp"]], inc=True)
            state["prev_kcb"] = tkc
            state["kch_user"] = tkc
            if kd == 15:
                state["last_amax_kcb"] = tkc
            tps = []
            for half in range(2):
                for j in range(8):
                    kf = half * 8 + j
                    t = sch.op("PE", lambda e, kf=kf, half=half, j=j: e.transpose(
                        TRP2[half][:, j * 128:(j + 1) * 128],
                        KCB[:, kf * 128:(kf + 1) * 128], IDT[:]),
                        deps=([tkc, state["prev_red"][half]] if j == 0 else []),
                        inc=(j == 7))
                    if j == 7:
                        tps.append(t)
            state["last_tp"] = tps[1]
            for half in range(2):
                state["prev_red"][half] = sch.op("DVE", lambda e, half=half: e.tensor_reduce(
                    AMP[:, half * 8:(half + 1) * 8],
                    TRP2[half][:].rearrange("p (j d) -> p j d", j=8),
                    AX.X, ALU.max, apply_absolute_value=True),
                    deps=[tps[half]], inc=True)
            if (alvl % 100) < 5:
                state["amx_fin"] = state["prev_red"][1]
                return
            if kd == 0:
                state["amx_fin"] = sch.op("DVE", lambda e: e.tensor_copy(AMX[:], AMP[:]),
                                          inc=True)
            else:
                state["amx_fin"] = sch.op("DVE", lambda e: e.tensor_tensor(
                    AMX[:], AMX[:], AMP[:], ALU.max), inc=True)

        def emit_kq_scales(ts):
            t1 = sch.op("DVE", lambda e: e._custom_dve(
                DIV127G, out=SCLF[:], in0=AMX[:], in1=None,
                s0=1.0 / QMAX, s1=1e-30, imm2=0.0),
                deps=[state["amx_fin"]], inc=True)
            t3 = sch.op("SP", lambda e: e.dma_start(
                out=amx_sc.ap()[1].rearrange("kf p -> p kf"), in_=SCLF[:],
                allow_slow_non_contiguous=True), deps=[t1], inc=True)
            RT = [KCB[:, 0:512], KCB[:, 512:1024]]
            tm0 = sch.op("DVE", lambda e, RT=RT: e.memset(RT[0][:], 0.0),
                         deps=[state["prev_red"][0], state["prev_red"][1], state["last_tp"]], inc=True)
            tm1 = sch.op("DVE", lambda e, RT=RT: e.memset(RT[1][:], 0.0), inc=True)
            prev_mm = {0: None, 1: None}
            prev_ev = None
            for s in range(4):
                rr = s % 2
                t4 = sch.op("GP", lambda e, s=s, rr=rr, RT=RT: e.dma_start(
                    out=RT[rr][0:1, :],
                    in_=amx_sc.ap()[1].rearrange("kf p -> (kf p)")[s * 512:(s + 1) * 512]),
                    deps=[t3, tm0, tm1, prev_mm[rr]], inc=True)
                prev_mm[rr] = sch.op("PE", lambda e, rr=rr, RT=RT: e.matmul(
                    OP[2][:, 0:512], ONE[:], RT[rr][:], start=True, stop=True),
                    deps=[t4, tok_one, tok_ev.get((ts - 1, 0)), prev_ev], inc=True)
                prev_ev = sch.op("DVE", lambda e, s=s: e.tensor_copy(
                    SRP[:, s * 512:(s + 1) * 512], OP[2][:, 0:512]),
                    deps=[prev_mm[rr]], inc=True)
            state["srp"] = prev_ev
            state["irp"] = sch.op("DVE", lambda e: e._custom_dve(
                RECIP_PS, out=IRP[:], in0=SRP[:], in1=None, **RECIP_CONSTS),
                deps=[prev_ev], inc=True)
            state["ev_op2"] = prev_ev
            state["rt_done"] = [prev_mm[0], prev_mm[1]]

        def emit_kq_quant(kd):
            dep = [state["kch_user"]] if kd > 0 else [state["last_amax_kcb"]]
            tl = sch.op("SP", lambda e, kd=kd: e.dma_start(
                out=KCH[:], in_=kern_d.rearrange("(kd p) f -> p kd f", p=128)[:, kd, :]),
                deps=dep, inc=True)
            dep2 = [tl, state["prev_kq"]] + (
                [state["last_tp"]] + state.get("rt_done", []) if kd == 0 else [])
            tkc = sch.op("DVE", lambda e: e.tensor_copy(KCB[:], KCH[:]), deps=dep2, inc=True)
            state["prev_quant_kcb"] = sch.op(
                "DVE", lambda e: e._custom_dve(
                    RNE_FQT, out=KCH[:], in0=KCB[:], in1=IRP[:],
                    s0=0.0, s1=0.0, imm2=M_ROUND),
                deps=[state["irp"]], inc=True)
            tq = sch.op("DVE", lambda e: e.tensor_tensor(KCB[:], KCH[:], SRP[:], ALU.mult),
                        deps=[state["srp"]], inc=True)
            state["kch_user"] = tq
            state["prev_kq"] = sch.op("SP", lambda e, kd=kd: e.dma_start(
                out=kq_sc.ap()[kd], in_=KCB[:]), deps=[tq], inc=True)

        tok_x2 = {}
        tok_ev = {}
        tok_tln = {}
        tok_mm_last = None
        tok_gwg = None
        prev_x2dma = None
        for ts in range(NTS):
            g, tl = ts // 4, ts % 4
            if tl == 0 and "m" in bsel:
                dep_g = [tok_mm_last] if ts > 0 else []
                tok_gwg_h = []
                for hf in range(2):
                    tok_gwg_h.append(sch.op("SP", lambda e, g=g, hf=hf: e.dma_start(
                        out=GWG[:].rearrange("p (cc t) -> p cc t", cc=NCC)[:, hf * 16:(hf + 1) * 16, :],
                        in_=gw_sc.ap()[g].rearrange("cc p t -> p cc t")[:, hf * 16:(hf + 1) * 16, :]),
                        deps=dep_g, inc=True))
            if ts < 8:
                if "a" in bsel:
                    nmax = alvl // 100 if alvl >= 100 else 16
                    if 2 * ts < nmax:
                        emit_kq_amax(2 * ts)
                    if 2 * ts + 1 < nmax:
                        emit_kq_amax(2 * ts + 1)
            elif ts == 8:
                if "s" in bsel:
                    emit_kq_scales(ts)
            else:
                if "q" in bsel:
                    for kd in range((ts - 9) * 3, min((ts - 8) * 3, 16)):
                        emit_kq_quant(kd)
            if "m" not in bsel:
                continue
            x_tok = sch.op("SP", lambda e, ts=ts: e.dma_start(
                out=XT[:], in_=x_d[ts * 128:(ts + 1) * 128, :]),
                deps=[prev_x2dma], inc=True)
            ev_toks = []
            for eh in range(2):
                rop = (2 * ts + eh) % 3
                pts, peh = divmod(2 * ts + eh - 3, 2)
                dep0 = [tok_gwg_h[0], tok_ev.get((pts, peh))] + tok_wo
                if ts == 0 and eh == 0:
                    dep0.append(tok_ssq_done)
                if rop == 2 and state["ev_op2"] is not None:
                    dep0.append(state["ev_op2"])
                tok_mm = None
                for cc in range(NCC):
                    dcc = (dep0 if cc == 0 else ([tok_gwg_h[1]] if cc == 16 else []))
                    for sub in range(2):
                        tok_mm = sch.op("PE", lambda e, eh=eh, cc=cc, sub=sub, tl=tl, rop=rop: e.matmul(
                            OP[rop][:, sub * 512:(sub + 1) * 512],
                            GWG[:, cc * 512 + tl * 128: cc * 512 + (tl + 1) * 128],
                            WO[:, cc * E + eh * 1024 + sub * 512: cc * E + eh * 1024 + (sub + 1) * 512],
                            start=(cc == 0), stop=(cc == NCC - 1)),
                            deps=(dcc if sub == 0 else []),
                            inc=(cc == NCC - 1 and sub == 1))
                tok_mm_last = tok_mm
                tev = sch.op("DVE", lambda e, eh=eh, ts=ts, rop=rop: e.tensor_scalar(
                    ATN[:, eh * 1024:(eh + 1) * 1024], OP[rop][:],
                    RSTD[:, ts:ts + 1], None, ALU.mult),
                    deps=[tok_mm, tok_rstd, tok_tln.get(ts - 1)], inc=True)
                tok_ev[(ts, eh)] = tev
                ev_toks.append(tev)
            tsq0 = sch.op("ACT", lambda e: e.activation(
                SQJ[:], ATN[:, 0:E // 2], AF.Square, accum_out=BST[:, 5:6]),
                deps=[ev_toks[0]], inc=True)
            tsq = sch.op("ACT", lambda e: e.activation(
                SQJ[:], ATN[:, E // 2:E], AF.Square, accum_out=BST[:, 0:1]),
                deps=[ev_toks[1]], inc=True)
            sch.op("DVE", lambda e: e.tensor_reduce(
                BST[:, 1:2], ATN[:], AX.X, ALU.add, negate=True), deps=ev_toks)
            sch.op("DVE", lambda e: e.tensor_scalar(BST[:, 1:2], BST[:, 1:2], 1.0 / E, None, ALU.mult))
            sch.op("DVE", lambda e: e.tensor_tensor(BST[:, 2:3], BST[:, 1:2], BST[:, 1:2], ALU.mult))
            sch.op("DVE", lambda e: e.tensor_tensor(
                BST[:, 0:1], BST[:, 0:1], BST[:, 5:6], ALU.add), deps=[tsq0, tsq])
            sch.op("DVE", lambda e: e.tensor_scalar(BST[:, 3:4], BST[:, 0:1], 1.0 / E, None, ALU.mult))
            tvar = sch.op("DVE", lambda e: e.tensor_tensor(
                BST[:, 3:4], BST[:, 3:4], BST[:, 2:3], ALU.subtract), inc=True)
            sch.op("ACT", lambda e: e.activation(BST[:, 4:5], BST[:, 3:4], AF.Ln, bias=CONSTS[:, 0:1]),
                   deps=[tvar])
            trs = sch.op("ACT", lambda e: e.activation(
                BST[:, 4:5], BST[:, 4:5], AF.Exp, scale=-0.5), inc=True)
            tln = sch.op("DVE", lambda e: e.tensor_scalar(
                ATN[:], ATN[:], BST[:, 1:2], BST[:, 4:5], ALU.add, ALU.mult),
                deps=[trs, tsq0, tsq], inc=True)
            tok_tln[ts] = tln
            tx2 = sch.op("DVE", lambda e: e.tensor_tensor(XT[:], XT[:], ATN[:], ALU.add),
                         deps=[x_tok, tln], inc=True)
            tok_x2[ts] = tx2
            prev_x2dma = sch.op("SP", lambda e, ts=ts: e.dma_start(
                out=x2_sc.ap()[ts], in_=XT[:]), deps=[tx2], inc=True)

        esBP.close()
        esB.close()
        sch.barrier(real_ops={
            "PE": lambda en: en.ldweights(IDT[:]),
            "DVE": lambda en: en.tensor_copy(CONSTS[:, 3:4], CONSTS[:, 0:1]),
            "ACT": lambda en: en.activation(CONSTS[:, 4:5], CONSTS[:, 0:1], AF.Copy),
        })

        if "C" not in phases:
            sch.op("SP", lambda e: e.wait_ge(sch.sems["SP"], sch.count["SP"]))
            _replay_all(sch, block)
            return nc
        esC = ExitStack()
        KQ = esC.enter_context(nc.sbuf_tensor("KQ", [128, KE * F], BF16))
        X2C = [esC.enter_context(nc.sbuf_tensor(f"X2C{i}", [128, E], F32)) for i in range(3)]
        BQR = esC.enter_context(nc.sbuf_tensor("BQR", [128, F], F32))
        LNQ = esC.enter_context(nc.sbuf_tensor("LNQ", [128, E], F32))
        LNY = esC.enter_context(nc.sbuf_tensor("LNY", [128, F], F32))
        XQB = [esC.enter_context(nc.sbuf_tensor(f"XQB{i}", [128, E], BF16)) for i in range(2)]
        XQT = [esC.enter_context(nc.sbuf_tensor(f"XQT{i}", [128, E], BF16)) for i in range(2)]
        Y1 = esC.enter_context(nc.sbuf_tensor("Y1", [128, F], F32))
        CST = esC.enter_context(nc.sbuf_tensor("CST", [128, 16], F32))
        BQC = esC.enter_context(nc.sbuf_tensor("BQC", [1, 4], F32))
        BRW = esC.enter_context(nc.sbuf_tensor("BRW", [1, F], F32))
        BRB = esC.enter_context(nc.sbuf_tensor("BRB", [1, F], BF16))
        RT2 = esC.enter_context(nc.sbuf_tensor("RT2", [128, 512], BF16))

        esCP = ExitStack()
        YP = [esCP.enter_context(nc.psum_tensor(f"YP{i}", [128, 1024], F32)) for i in range(3)]
        TRP3 = [esCP.enter_context(nc.psum_tensor(f"TRP3{i}", [128, 1024], BF16)) for i in range(2)]

        tok_kq_each = []
        for kd in range(KE):
            tok_kq_each.append(sch.op("SP", lambda e, kd=kd: e.dma_start(
                out=KQ[:, kd * F:(kd + 1) * F], in_=kq_sc.ap()[kd]), inc=True))
        tok_kq_all = sch.queue_toks("SP")

        tb = sch.op("SP", lambda e: e.dma_start(out=BRW[:], in_=bias_d), inc=True)
        sch.op("DVE", lambda e: e.tensor_reduce(
            BQC[0:1, 0:1], BRW[:], AX.X, ALU.max, apply_absolute_value=True), deps=[tb])
        sch.op("DVE", lambda e: e._custom_dve(
            DIV127G, out=BQC[0:1, 2:3], in0=BQC[0:1, 0:1], in1=None,
            s0=1.0 / QMAX, s1=1e-30, imm2=0.0))
        sch.op("DVE", lambda e: e.reciprocal(BQC[0:1, 1:2], BQC[0:1, 2:3]))
        tbq = sch.op("DVE", lambda e: e._custom_dve(
            RNE_FQS, out=BRB[:], in0=BRW[:], in1=None,
            s0=BQC[0:1, 2:3], s1=BQC[0:1, 1:2], imm2=M_ROUND), inc=True)
        tm2 = sch.op("DVE", lambda e: e.memset(RT2[:], 0.0), inc=True)
        tok_bqr = None
        prev_mm2 = None
        prev_bev = None
        for s in range(4):
            tc = sch.op("DVE", lambda e, s=s: e.tensor_copy(
                RT2[0:1, :], BRB[0:1, s * 512:(s + 1) * 512]),
                deps=[tbq, tm2, prev_mm2], inc=True)
            prev_mm2 = sch.op("PE", lambda e: e.matmul(
                YP[2][:, 0:512], ONE[:], RT2[:], start=True, stop=True),
                deps=[tc, tok_one, prev_bev], inc=True)
            prev_bev = tok_bqr = sch.op("DVE", lambda e, s=s: e.tensor_copy(
                BQR[:, s * 512:(s + 1) * 512], YP[2][:, 0:512]), deps=[prev_mm2], inc=True)

        last_x2_reader = [None, None, None]
        tok_txq = {}
        tok_tps = {}
        tok_tevt = {}
        tok_ymm = {}
        tok_tadd = {}
        prev_out_dma = None
        prev_epi_ydve = None

        def emit_prep(ts):
            r3 = ts % 3
            rp = ts % 2
            tx = sch.op("SP", lambda e, ts=ts, r3=r3: e.dma_start(
                out=X2C[r3][:], in_=x2_sc.ap()[ts]), deps=[last_x2_reader[r3]], inc=True)
            tsq = sch.op("ACT", lambda e, r3=r3: e.activation(
                LNQ[:], X2C[r3][:], AF.Square, accum_out=CST[:, 0:1]),
                deps=[tx, tok_txq.get(ts - 1)], inc=True)
            sch.op("DVE", lambda e, r3=r3: e.tensor_reduce(
                CST[:, 1:2], X2C[r3][:], AX.X, ALU.add, negate=True), deps=[tx])
            sch.op("DVE", lambda e: e.tensor_scalar(CST[:, 1:2], CST[:, 1:2], 1.0 / E, None, ALU.mult))
            sch.op("DVE", lambda e: e.tensor_tensor(CST[:, 2:3], CST[:, 1:2], CST[:, 1:2], ALU.mult))
            sch.op("DVE", lambda e: e.tensor_scalar(CST[:, 3:4], CST[:, 0:1], 1.0 / E, None, ALU.mult),
                   deps=[tsq])
            tvar = sch.op("DVE", lambda e: e.tensor_tensor(
                CST[:, 3:4], CST[:, 3:4], CST[:, 2:3], ALU.subtract), inc=True)
            sch.op("ACT", lambda e: e.activation(CST[:, 4:5], CST[:, 3:4], AF.Ln, bias=CONSTS[:, 0:1]),
                   deps=[tvar])
            trs = sch.op("ACT", lambda e: e.activation(
                CST[:, 4:5], CST[:, 4:5], AF.Exp, scale=-0.5), inc=True)
            tln = sch.op("DVE", lambda e, r3=r3: e.tensor_scalar(
                LNQ[:], X2C[r3][:], CST[:, 1:2], CST[:, 4:5], ALU.add, ALU.mult),
                deps=[trs, tsq], inc=True)
            sch.op("DVE", lambda e: e.tensor_reduce(
                CST[:, 5:6], LNQ[:], AX.X, ALU.max, apply_absolute_value=True))
            sch.op("DVE", lambda e: e._custom_dve(
                DIV127G, out=CST[:, 7:8], in0=CST[:, 5:6], in1=None,
                s0=1.0 / QMAX, s1=1e-30, imm2=0.0))
            sch.op("DVE", lambda e: e.reciprocal(CST[:, 6:7], CST[:, 7:8]))
            tok_txq[ts] = sch.op("DVE", lambda e, rp=rp: e._custom_dve(
                RNE_FQS, out=XQB[rp][:], in0=LNQ[:], in1=None,
                s0=CST[:, 7:8], s1=CST[:, 6:7], imm2=M_ROUND),
                deps=[tok_tps.get(ts - 2)], inc=True)

        def emit_compute(ts):
            rp = ts % 2
            tps = []
            for half in range(2):
                for j in range(8):
                    kd = half * 8 + j
                    t = sch.op("PE", lambda e, kd=kd, half=half, j=j, rp=rp: e.transpose(
                        TRP3[half][:, j * 128:(j + 1) * 128],
                        XQB[rp][:, kd * 128:(kd + 1) * 128], IDT[:]),
                        deps=([tok_txq[ts], tok_tevt.get((ts - 1, half)), tok_idt]
                              if j == 0 else []),
                        inc=(j == 7))
                    if j == 7:
                        tps.append(t)
            tok_tps[ts] = tps[1]
            tevt = []
            for half in range(2):
                tt_ = sch.op("DVE", lambda e, half=half, rp=rp: e.tensor_copy(
                    XQT[rp][:].rearrange("p (kd t) -> p kd t", kd=KE)[:, half * 8:(half + 1) * 8, :],
                    TRP3[half][:].rearrange("p (j t) -> p j t", j=8)),
                    deps=[tps[half], tok_ymm.get(ts - 2)], inc=True)
                tok_tevt[(ts, half)] = tt_
                tevt.append(tt_)
            last_ymm = None
            for fh in range(2):
                rop = (2 * ts + fh) % 3
                dep0 = tevt + tok_kq_all + [tok_bqr, tok_tadd.get(2 * ts + fh - 3)]
                for kd in range(KE):
                    for sub in range(2):
                        last_ymm = sch.op("PE", lambda e, fh=fh, kd=kd, sub=sub, rop=rop, rp=rp: e.matmul(
                            YP[rop][:, sub * 512:(sub + 1) * 512],
                            XQT[rp][:, kd * 128:(kd + 1) * 128],
                            KQ[:, kd * F + fh * 1024 + sub * 512: kd * F + fh * 1024 + (sub + 1) * 512],
                            start=(kd == 0), stop=(kd == KE - 1)),
                            deps=(dep0 if (kd == 0 and sub == 0) else []),
                            inc=(kd == KE - 1 and sub == 1))
                tok_ymm[(ts, fh)] = last_ymm
            tok_ymm[ts] = last_ymm

        def emit_epi(ts):
            nonlocal_vars = None
            r3 = ts % 3
            res_toks = []
            for fh in range(2):
                rop = (2 * ts + fh) % 3
                tadd = sch.op("DVE", lambda e, fh=fh, rop=rop: e.tensor_tensor(
                    Y1[:, fh * 1024:(fh + 1) * 1024], YP[rop][:],
                    BQR[:, fh * 1024:(fh + 1) * 1024], ALU.add),
                    deps=[tok_ymm[(ts, fh)], tok_bqr], inc=True)
                tok_tadd[2 * ts + fh] = tadd
                res_toks.append(tadd)
            trelu = sch.op("DVE", lambda e: e.tensor_scalar_max(Y1[:], Y1[:], 0.0),
                           deps=res_toks, inc=True)
            sch.op("DVE", lambda e: e.tensor_reduce(
                CST[:, 13:14], Y1[:], AX.X, ALU.max, apply_absolute_value=True),
                deps=[trelu])
            sch.op("DVE", lambda e: e._custom_dve(
                DIV127G, out=CST[:, 15:16], in0=CST[:, 13:14], in1=None,
                s0=1.0 / QMAX, s1=1e-30, imm2=0.0))
            sch.op("DVE", lambda e: e.reciprocal(CST[:, 14:15], CST[:, 15:16]))
            tfq = sch.op("DVE", lambda e: e._custom_dve(
                RNE_FQS, out=Y1[:], in0=Y1[:], in1=None,
                s0=CST[:, 15:16], s1=CST[:, 14:15], imm2=M_ROUND),
                deps=[trelu], inc=True)
            tsq2 = sch.op("ACT", lambda e: e.activation(
                LNY[:], Y1[:], AF.Square, accum_out=CST[:, 8:9]),
                deps=[tfq, prev_out_dma], inc=True)
            sch.op("DVE", lambda e: e.tensor_reduce(
                CST[:, 9:10], Y1[:], AX.X, ALU.add, negate=True), deps=[tfq])
            sch.op("DVE", lambda e: e.tensor_scalar(CST[:, 9:10], CST[:, 9:10], 1.0 / F, None, ALU.mult))
            sch.op("DVE", lambda e: e.tensor_tensor(CST[:, 10:11], CST[:, 9:10], CST[:, 9:10], ALU.mult))
            sch.op("DVE", lambda e: e.tensor_scalar(CST[:, 11:12], CST[:, 8:9], 1.0 / F, None, ALU.mult),
                   deps=[tsq2])
            tvar2 = sch.op("DVE", lambda e: e.tensor_tensor(
                CST[:, 11:12], CST[:, 11:12], CST[:, 10:11], ALU.subtract), inc=True)
            sch.op("ACT", lambda e: e.activation(CST[:, 12:13], CST[:, 11:12], AF.Ln, bias=CONSTS[:, 0:1]),
                   deps=[tvar2])
            trs2 = sch.op("ACT", lambda e: e.activation(
                CST[:, 12:13], CST[:, 12:13], AF.Exp, scale=-0.5), inc=True)
            tln2 = sch.op("DVE", lambda e: e.tensor_scalar(
                LNY[:], Y1[:], CST[:, 9:10], CST[:, 12:13], ALU.add, ALU.mult),
                deps=[trs2, tsq2], inc=True)
            tout = sch.op("DVE", lambda e, r3=r3: e.tensor_tensor(
                LNY[:], LNY[:], X2C[r3][:], ALU.add), inc=True)
            last_x2_reader[r3] = tout
            return sch.op("SP", lambda e, ts=ts: e.dma_start(
                out=y_d[ts * 128:(ts + 1) * 128, :], in_=LNY[:]), deps=[tout], inc=True)

        for k in range(NTS + 2):
            if k < NTS:
                emit_prep(k)
            if k >= 2:
                prev_out_dma = emit_epi(k - 2)
            if 1 <= k <= NTS:
                emit_compute(k - 1)

        esCP.close()
        esC.close()

        def _final_waits(e):
            last = None
            for q in ("SP", "GP"):
                for r in range(NRING):
                    n = f"{q}{r}"
                    if sch.count[n] > 0:
                        last = e.wait_ge(sch.sems[n], sch.count[n])
            return last
        sch.ops["SP"].append(((), lambda e: _final_waits(e), None))
        _replay_all(sch, block)
    return nc


def _replay_all(sch, block):
    @block.sync
    def _(sync):
        sch.replay("SP", sync)

    @block.gpsimd
    def _(gpsimd):
        sch.replay("GP", gpsimd)

    @block.tensor
    def _(tensor):
        sch.replay("PE", tensor)

    @block.vector
    def _(vector):
        sch.replay("DVE", vector)

    @block.scalar
    def _(scalar):
        sch.replay("ACT", scalar)


_NC_CACHE = None


def kernel(L0_attn_x, L0_attn_Wq1, L0_attn_Wq2, L0_attn_Wk1, L0_attn_Wk2,
           L0_attn_Wv, L0_attn_Wo, L0_attn_norm_w, L0_ffn_kernel, L0_ffn_bias):
    global _NC_CACHE
    x = np.ascontiguousarray(np.asarray(L0_attn_x, dtype=np.float32).reshape(NCORES * T, E))
    wq1 = np.ascontiguousarray(np.asarray(L0_attn_Wq1, dtype=np.float32))
    wq2 = np.ascontiguousarray(np.asarray(L0_attn_Wq2, dtype=np.float32))
    wk1 = np.ascontiguousarray(np.asarray(L0_attn_Wk1, dtype=np.float32))
    wk2 = np.ascontiguousarray(np.asarray(L0_attn_Wk2, dtype=np.float32))
    wv = np.ascontiguousarray(np.asarray(L0_attn_Wv, dtype=np.float32))
    wo = np.ascontiguousarray(np.asarray(L0_attn_Wo, dtype=np.float32))
    nw = np.ascontiguousarray(np.asarray(L0_attn_norm_w, dtype=np.float32).reshape(NCC, 128))
    kern = np.ascontiguousarray(np.asarray(L0_ffn_kernel, dtype=np.float32))
    bias = np.ascontiguousarray(np.asarray(L0_ffn_bias, dtype=np.float32).reshape(1, F))
    ident = np.eye(128, dtype=ml_dtypes.bfloat16)
    ones = np.ones((128, 128), dtype=ml_dtypes.bfloat16)

    if _NC_CACHE is None:
        _NC_CACHE = build_program()
    nc = _NC_CACHE

    in_maps = []
    for c in range(NCORES):
        xs = np.ascontiguousarray(x[c * T:(c + 1) * T])
        if c % 2 == 1:
            xbs = np.ascontiguousarray(x[c * T - NB: c * T])
        else:
            xbs = np.zeros((NB, E), np.float32)
        in_maps.append(dict(x=xs, xb=xbs, wq1=wq1, wk1=wk1, wq2=wq2, wk2=wk2,
                            wv=wv, wo=wo, kern=kern, biasr=bias, nwr=nw,
                            identb=ident, onesb=ones))
    res = run_bass_kernel_spmd(nc, in_maps, core_ids=list(range(NCORES)))
    out = np.concatenate([res.results[c]["y"] for c in range(NCORES)], axis=0)
    return out.reshape(4, 4096, E).astype(np.float32)


# version 08
# speedup vs baseline: 1.0042x; 1.0042x over previous
import numpy as np
import ml_dtypes
from contextlib import ExitStack

import concourse.bass as bass
import concourse.mybir as mybir
from concourse.bass_utils import run_bass_kernel_spmd


def _register_fq_ops():
    from concourse import dve_ops as D
    from concourse.dve_op import DveOp
    from concourse.dve_spec import (Spec, Src0, Src1, C0, C1, C2, lower,
                                    maxx, AluOp, Bin)

    have = {o.name: o for o in D.OPS}
    if "RNE_FQS" in have:
        return tuple(have[n] for n in ("DIV127G", "RNE_FQS", "RECIP_PS", "RNE_FQT"))

    donors = iter(("DVEHASH32L_SEED", "DVEHASH32L_BODY", "DVEHASH32L_DRAIN",
                   "TENSOR_MASK_FP8"))

    def _mk(name, body, ref):
        spec = Spec(body=body)
        sha = lower(spec, ver="v3", perf_en=False, name=name).sha("v3")
        op = DveOp(name, spec, subdim=False, uops_sha={"v3": sha}, reference=ref)
        D.OPS.append(op)
        D.CUSTOM_DVE_SPECS[name] = spec
        D.CUSTOM_DVE_REFS[name] = op.ref
        D._SUB_OPCODE_FOR_NAME[name] = D._SUB_OPCODE_FOR_NAME[next(donors)]
        return op

    def _f64(a):
        return a.dtype == np.float64

    div127 = _mk("DIV127G", maxx(Src0, C1) * C0,
                 lambda in0, in1, s0, s1, imm2:
                 np.maximum(in0, s1) / 127.0 if _f64(in0)
                 else np.maximum(in0, np.float32(s1)) * np.float32(s0))
    fqs = _mk("RNE_FQS", ((Src0 * C1 + C2) - C2) * C0,
              lambda in0, in1, s0, s1, imm2:
              np.round(in0 * (1.0 / s0)) * s0 if _f64(in0)
              else np.round(in0 * s1) * s0)
    notx = Bin(AluOp.BITWISE_NOT, Src0, Src0)
    y0 = notx * C0
    y1 = y0 * (C1 - Src0 * y0)

    def _ref_recip_pass(in0, in1, s0, s1, imm2):
        if _f64(in0):
            return in0.copy()
        not_x = (~in0.view(np.int32)).view(np.float32)
        y0 = not_x * np.float32(s0)
        y1 = y0 * (np.float32(s1) - in0 * y0)
        return y1 * (np.float32(imm2) - in0 * y1)

    rp = _mk("RECIP_PS", y1 * (C2 - Src0 * y1), _ref_recip_pass)
    fqt = _mk("RNE_FQT", (Src0 * Src1 + C2) - C2,
              lambda in0, in1, s0, s1, imm2:
              np.round(in0 * (1.0 / in1)) if _f64(in0)
              else np.round(in0 * in1))
    return div127, fqs, rp, fqt


RECIP_CONSTS = dict(s0=-0.23549792, s1=2.0017324, imm2=2.0)
DIV127G, RNE_FQS, RECIP_PS, RNE_FQT = _register_fq_ops()

F32 = mybir.dt.float32
BF16 = mybir.dt.bfloat16
AF = mybir.ActivationFunctionType
ALU = mybir.AluOpType
AX = mybir.AxisListType

NCORES = 8
T = 2048
NTS = 16
G = 4
E = 2048
KE = 16
C = 4096
NCC = 32
HD = 128
F = 2048
NB = 256
M_ROUND = 12582912.0
QMAX = 127.0


def _flt(deps):
    return tuple(d for d in deps if d is not None)


NRING = 8


class Sched:

    ENGINES = ("SP", "GP", "PE", "DVE", "ACT")

    def __init__(self, nc):
        self.nc = nc
        self.ops = {e: [] for e in self.ENGINES}
        self.count = {}
        self.dma_i = {"SP": 0, "GP": 0}
        self.sems = {}
        self.semnames = []
        for e in ("PE", "DVE", "ACT", "PL"):
            self.semnames.append(e)
            self.count[e] = 0
        for q in ("SP", "GP"):
            for r in range(NRING):
                n = f"{q}{r}"
                self.semnames.append(n)
                self.count[n] = 0

    def alloc_sems(self, es):
        for n in self.semnames:
            self.sems[n] = es.enter_context(self.nc.semaphore(f"s_{n}"))

    def op(self, eng, fn, deps=(), inc=False):
        deps = list(_flt(deps))
        if eng == "PLC":
            self.count["PL"] += 1
            tok = ("PL", self.count["PL"])
            self.ops["GP"].append((tuple(deps), fn, tok))
            return tok
        if eng in ("DVE", "ACT"):
            inc = True
        tok = None
        if eng in ("SP", "GP"):
            i = self.dma_i[eng]
            self.dma_i[eng] = i + 1
            sem = f"{eng}{i % NRING}"
            if self.count[sem] > 0:
                deps.append((sem, self.count[sem]))
            self.count[sem] += 16
            tok = (sem, self.count[sem])
        elif inc:
            self.count[eng] += 1
            tok = (eng, self.count[eng])
        self.ops[eng].append((tuple(deps), fn, tok))
        return tok

    def queue_toks(self, q):
        return [(f"{q}{r}", self.count[f"{q}{r}"]) for r in range(NRING)]

    def barrier(self, real_ops):
        toks = [("PL", self.count["PL"])]
        for e in ("PE", "DVE", "ACT"):
            toks.append(self.op(e, real_ops[e], inc=True))
        toks += self.queue_toks("SP") + self.queue_toks("GP")
        for e in self.ENGINES:
            self.ops[e].append((tuple(toks), None, None))
        return toks

    def replay(self, eng_name, engine):
        waited = {}
        self_serial = eng_name in ("DVE", "ACT")
        prev_count = 0
        prev_plc = 0
        for deps, fn, tok in self.ops[eng_name]:
            for (sname, val) in deps:
                if sname == eng_name or val <= 0:
                    continue
                if waited.get(sname, 0) < val:
                    engine.wait_ge(self.sems[sname], val)
                    waited[sname] = val
            if fn is None:
                continue
            is_plc = tok is not None and tok[0] == "PL"
            if self_serial and prev_count > 0:
                engine.wait_ge(self.sems[eng_name], prev_count)
            if is_plc and prev_plc > 0 and waited.get("PL", 0) < prev_plc:
                engine.wait_ge(self.sems["PL"], prev_plc)
                waited["PL"] = max(waited.get("PL", 0), prev_plc)
            inst = fn(engine)
            if tok is not None:
                step = 16 if tok[0] not in ("PE", "DVE", "ACT", "PL") else 1
                inst.then_inc(self.sems[tok[0]], step)
                if tok[0] == eng_name:
                    prev_count = tok[1]
                if is_plc:
                    prev_plc = tok[1]


def build_program(phases="ABC", bsel="wasqm", alvl=9):
    nc = _build_program(phases, bsel, alvl)
    mybir.codegen_inst_isa_subclasses(nc)
    return nc


def _build_program(phases="ABC", bsel="wasqm", alvl=9):
    _alvl_n = alvl
    nc = bass.Bass("TRN2", target_bir_lowering=False)

    x_d = nc.dram_tensor("x", [T, E], F32, kind="ExternalInput").ap()
    xb_d = nc.dram_tensor("xb", [NB, E], F32, kind="ExternalInput").ap()
    wq1_d = nc.dram_tensor("wq1", [E, HD], F32, kind="ExternalInput").ap()
    wk1_d = nc.dram_tensor("wk1", [E, HD], F32, kind="ExternalInput").ap()
    wq2_d = nc.dram_tensor("wq2", [HD, C], F32, kind="ExternalInput").ap()
    wk2_d = nc.dram_tensor("wk2", [HD, C], F32, kind="ExternalInput").ap()
    wv_d = nc.dram_tensor("wv", [E, C], F32, kind="ExternalInput").ap()
    wo_d = nc.dram_tensor("wo", [C, E], F32, kind="ExternalInput").ap()
    kern_d = nc.dram_tensor("kern", [E, F], F32, kind="ExternalInput").ap()
    bias_d = nc.dram_tensor("biasr", [1, F], F32, kind="ExternalInput").ap()
    nw_d = nc.dram_tensor("nwr", [NCC, 128], F32, kind="ExternalInput").ap()
    identb_d = nc.dram_tensor("identb", [128, 128], BF16, kind="ExternalInput").ap()
    onesb_d = nc.dram_tensor("onesb", [128, 128], BF16, kind="ExternalInput").ap()
    y_d = nc.dram_tensor("y", [T, E], F32, kind="ExternalOutput").ap()

    gw_sc = nc.dram_tensor("gw_sc", [G, NCC, 128, 512], BF16)
    x2_sc = nc.dram_tensor("x2_sc", [NTS, 128, E], F32)
    kq_sc = nc.dram_tensor("kq_sc", [KE, 128, F], BF16)
    ssq_sc = nc.dram_tensor("ssq_sc", [G, 512], F32)
    amx_sc = nc.dram_tensor("amx_sc", [2, 16, 128], F32)

    sch = Sched(nc)
    es = ExitStack()
    sch.alloc_sems(es)

    IDT = es.enter_context(nc.sbuf_tensor("IDT", [128, 128], BF16))
    ONE = es.enter_context(nc.sbuf_tensor("ONE", [128, 128], BF16))
    SSQTM = es.enter_context(nc.sbuf_tensor("SSQTM", [128, NTS], F32))
    CONSTS = es.enter_context(nc.sbuf_tensor("CONSTS", [128, 6], F32))

    with nc.Block() as block, es:
        esA = ExitStack()
        XNT = esA.enter_context(nc.sbuf_tensor("XNT", [128, KE * T], BF16))
        XNTB = esA.enter_context(nc.sbuf_tensor("XNTB", [128, KE * NB], BF16))
        WVS = [esA.enter_context(nc.sbuf_tensor(f"WVS{i}", [128, KE * 128], BF16)) for i in range(2)]
        WK2 = esA.enter_context(nc.sbuf_tensor("WK2", [128, C], BF16))
        WQ2 = esA.enter_context(nc.sbuf_tensor("WQ2", [128, C], BF16))
        QPT = esA.enter_context(nc.sbuf_tensor("QPT", [128, T], BF16))
        LPT = esA.enter_context(nc.sbuf_tensor("LPT", [128, T], BF16))
        LPTB = esA.enter_context(nc.sbuf_tensor("LPTB", [128, NB], BF16))
        NWC = esA.enter_context(nc.sbuf_tensor("NWC", [128, NCC], F32))
        H0C = esA.enter_context(nc.sbuf_tensor("H0C", [128, NCC], F32))
        SSQR = esA.enter_context(nc.sbuf_tensor("SSQR", [1, 512], F32))

        sch.op("DVE", lambda e: e.memset(CONSTS[:, 0:1], 1e-5))
        sch.op("DVE", lambda e: e.memset(CONSTS[:, 1:2], 1e-6))
        tok_consts = sch.op("DVE", lambda e: e.memset(CONSTS[:, 2:3], 1.0), inc=True)
        tok_idt = sch.op("SP", lambda e: e.dma_start(out=IDT[:], in_=identb_d), inc=True)
        tok_one = sch.op("SP", lambda e: e.dma_start(out=ONE[:], in_=onesb_d), inc=True)
        tok_nwc = sch.op("SP", lambda e: e.dma_start(
            out=NWC[:], in_=nw_d.rearrange("cc p -> p cc"),
            allow_slow_non_contiguous=True), inc=True)
        tok_wk2 = sch.op("GP", lambda e: e.dma_start(out=WK2[:], in_=wk2_d), inc=True)
        tok_wq2 = sch.op("GP", lambda e: e.dma_start(out=WQ2[:], in_=wq2_d), inc=True)

        if "A" in phases:
            esP = ExitStack()
            XP = [esP.enter_context(nc.sbuf_tensor(f"XP{i}", [128, E], F32)) for i in range(2)]
            WQ1 = esP.enter_context(nc.sbuf_tensor("WQ1", [128, KE * HD], BF16))
            WK1 = esP.enter_context(nc.sbuf_tensor("WK1", [128, KE * HD], BF16))
            XNB = [esP.enter_context(nc.sbuf_tensor(f"XNB{i}", [128, E], BF16)) for i in range(2)]
            PSS = esP.enter_context(nc.sbuf_tensor("PSS", [128, NTS + 2], F32))
            NMS = esP.enter_context(nc.sbuf_tensor("NMS", [128, NTS + 2], F32))
            RSS = esP.enter_context(nc.sbuf_tensor("RSS", [128, NTS + 2], F32))
            TV1 = esP.enter_context(nc.sbuf_tensor("TV1", [128, 1], F32))
            TV2 = esP.enter_context(nc.sbuf_tensor("TV2", [128, 1], F32))

            tok_wq1 = sch.op("GP", lambda e: e.dma_start(
                out=WQ1[:].rearrange("p (ke h) -> p ke h", ke=KE),
                in_=wq1_d.rearrange("(ke p) h -> p ke h", p=128)), inc=True)
            tok_wk1 = sch.op("GP", lambda e: e.dma_start(
                out=WK1[:].rearrange("p (ke h) -> p ke h", ke=KE),
                in_=wk1_d.rearrange("(ke p) h -> p ke h", p=128)), inc=True)

            def load_wvs(cc, deps):
                return sch.op("GP", lambda e, cc=cc: e.dma_start(
                    out=WVS[cc % 2][:].rearrange("p (ke q) -> p ke q", ke=KE),
                    in_=wv_d.rearrange("(ke p) c -> p ke c", p=128)[:, :, cc * 128:(cc + 1) * 128]),
                    deps=deps, inc=True)

            tok_wvs = {}
            tok_wvs[0] = load_wvs(0, [])
            tok_wvs[1] = load_wvs(1, [])

            esPP = ExitStack()
            TRP = [esPP.enter_context(nc.psum_tensor(f"TRP{i}", [128, 1024], BF16)) for i in range(2)]
            QPP = esPP.enter_context(nc.psum_tensor("QPP", [128, 128], F32))
            LPP = esPP.enter_context(nc.psum_tensor("LPP", [128, 128], F32))

            last_xp_reader = [None, None]
            prev_ev = [None, None]
            prev_tp_last = [None, None]
            prev_qev = None
            prev_lev = None
            prev_ln_tok = None
            last_bnd_ev = []
            tok_lptb_done = None
            stat_toks = {}

            def p_stats(ts):
                nonlocal prev_ln_tok
                r = ts % 2
                bnd = ts >= NTS
                src_ = (x_d[ts * 128:(ts + 1) * 128, :] if not bnd
                        else xb_d[(ts - NTS) * 128:(ts - NTS + 1) * 128, :])
                tok_x = sch.op("SP", lambda e, src_=src_, r=r: e.dma_start(
                    out=XP[r][:], in_=src_), deps=[last_xp_reader[r]], inc=True)
                tok_sq = sch.op("ACT", lambda e, r=r, ts=ts: e.activation(
                    XNB[r][:], XP[r][:], AF.Square, accum_out=PSS[:, ts:ts + 1]),
                    deps=[tok_x, prev_tp_last[r]], inc=True)
                sch.op("DVE", lambda e, r=r, ts=ts: e.tensor_reduce(
                    NMS[:, ts:ts + 1], XP[r][:], AX.X, ALU.add, negate=True),
                    deps=[tok_x, prev_ln_tok])
                sch.op("DVE", lambda e, ts=ts: e.tensor_scalar(
                    NMS[:, ts:ts + 1], NMS[:, ts:ts + 1], 1.0 / E, None, ALU.mult))
                sch.op("DVE", lambda e, ts=ts: e.tensor_tensor(
                    TV1[:], NMS[:, ts:ts + 1], NMS[:, ts:ts + 1], ALU.mult))
                sch.op("DVE", lambda e, ts=ts: e.tensor_scalar(
                    TV2[:], PSS[:, ts:ts + 1], 1.0 / E, None, ALU.mult), deps=[tok_sq])
                tok_v2 = sch.op("DVE", lambda e: e.tensor_tensor(
                    TV2[:], TV2[:], TV1[:], ALU.subtract), inc=True)
                prev_ln_tok = sch.op("ACT", lambda e, ts=ts: e.activation(
                    RSS[:, ts:ts + 1], TV2[:], AF.Ln, bias=CONSTS[:, 0:1]),
                    deps=[tok_v2, tok_consts], inc=True)
                tok_rs = sch.op("ACT", lambda e, ts=ts: e.activation(
                    RSS[:, ts:ts + 1], RSS[:, ts:ts + 1], AF.Exp, scale=-0.5), inc=True)
                stat_toks[ts] = (tok_sq, tok_rs)

            def p_finish(ts):
                nonlocal prev_qev, prev_lev, tok_lptb_done
                r = ts % 2
                bnd = ts >= NTS
                tok_sq, tok_rs = stat_toks[ts]
                tok_xn = sch.op("DVE", lambda e, r=r, ts=ts: e.tensor_scalar(
                    XNB[r][:], XP[r][:], NMS[:, ts:ts + 1], RSS[:, ts:ts + 1],
                    ALU.add, ALU.mult), deps=[tok_sq, tok_rs], inc=True)
                last_xp_reader[r] = tok_xn
                tok_tp = [None, None]
                for half in range(2):
                    for j in range(8):
                        ke = half * 8 + j
                        t = sch.op("PE", lambda e, ke=ke, half=half, j=j, r=r: e.transpose(
                            TRP[half][:, j * 128:(j + 1) * 128],
                            XNB[r][:, ke * 128:(ke + 1) * 128], IDT[:]),
                            deps=([tok_xn, tok_idt, prev_ev[half]] if j == 0 else []),
                            inc=(j == 7))
                        if j == 7:
                            tok_tp[half] = t
                prev_tp_last[r] = tok_tp[1]
                evs = []
                for half in range(2):
                    if not bnd:
                        dst = XNT[:].rearrange("p (ke t) -> p ke t", ke=KE)[
                            :, half * 8:(half + 1) * 8, ts * 128:(ts + 1) * 128]
                    else:
                        dst = XNTB[:].rearrange("p (ke t) -> p ke t", ke=KE)[
                            :, half * 8:(half + 1) * 8, (ts - NTS) * 128:(ts - NTS + 1) * 128]
                    ev = sch.op("DVE", lambda e, dst=dst, half=half: e.tensor_copy(
                        dst, TRP[half][:].rearrange("p (j t) -> p j t", j=8)),
                        deps=[tok_tp[half]], inc=True)
                    prev_ev[half] = ev
                    evs.append(ev)
                if bnd:
                    last_bnd_ev.extend(evs)
                if not bnd:
                    tq_mm = None
                    for ke in range(KE):
                        tq_mm = sch.op("PE", lambda e, ke=ke, ts=ts: e.matmul(
                            QPP[:], WQ1[:, ke * 128:(ke + 1) * 128],
                            XNT[:, ke * T + ts * 128: ke * T + (ts + 1) * 128],
                            start=(ke == 0), stop=(ke == KE - 1)),
                            deps=(evs + [tok_wq1, prev_qev] if ke == 0 else []),
                            inc=(ke == KE - 1))
                    tl_mm = None
                    for ke in range(KE):
                        tl_mm = sch.op("PE", lambda e, ke=ke, ts=ts: e.matmul(
                            LPP[:], WK1[:, ke * 128:(ke + 1) * 128],
                            XNT[:, ke * T + ts * 128: ke * T + (ts + 1) * 128],
                            start=(ke == 0), stop=(ke == KE - 1)),
                            deps=([tok_wk1, prev_lev] if ke == 0 else []),
                            inc=(ke == KE - 1))
                    prev_qev = sch.op("DVE", lambda e, ts=ts: e.tensor_copy(
                        QPT[:, ts * 128:(ts + 1) * 128], QPP[:]), deps=[tq_mm], inc=True)
                    prev_lev = sch.op("DVE", lambda e, ts=ts: e.tensor_copy(
                        LPT[:, ts * 128:(ts + 1) * 128], LPP[:]), deps=[tl_mm], inc=True)
                else:
                    tb_ = ts - NTS
                    tl_mm = None
                    for ke in range(KE):
                        tl_mm = sch.op("PE", lambda e, ke=ke, tb_=tb_: e.matmul(
                            LPP[:], WK1[:, ke * 128:(ke + 1) * 128],
                            XNTB[:, ke * NB + tb_ * 128: ke * NB + (tb_ + 1) * 128],
                            start=(ke == 0), stop=(ke == KE - 1)),
                            deps=(evs + [prev_lev] if ke == 0 else []),
                            inc=(ke == KE - 1))
                    prev_lev = tok_lptb_done = sch.op("DVE", lambda e, tb_=tb_: e.tensor_copy(
                        LPTB[:, tb_ * 128:(tb_ + 1) * 128], LPP[:]), deps=[tl_mm], inc=True)

            for k in range(NTS + 3):
                if k < NTS + 2:
                    p_stats(k)
                if k >= 1:
                    p_finish(k - 1)

            esPP.close()
            esP.close()

            esM = ExitStack()
            SCF = [esM.enter_context(nc.sbuf_tensor(f"SCF{i}", [128, 512], F32)) for i in range(2)]
            SCA = [esM.enter_context(nc.sbuf_tensor(f"SCA{i}", [128, 512], F32)) for i in range(2)]
            SCKV = [esM.enter_context(nc.sbuf_tensor(f"SCKV{i}", [128, 512], F32)) for i in range(2)]
            SCH_ = [esM.enter_context(nc.sbuf_tensor(f"SCH{i}", [128, 512], F32)) for i in range(2)]
            SCSQ = [esM.enter_context(nc.sbuf_tensor(f"SCSQ{i}", [128, 512], F32)) for i in range(2)]
            SCQ = [esM.enter_context(nc.sbuf_tensor(f"SCQ{i}", [128, 512], F32)) for i in range(2)]
            SCG = [esM.enter_context(nc.sbuf_tensor(f"SCG{i}", [128, 512], F32)) for i in range(2)]
            SCGQ = [esM.enter_context(nc.sbuf_tensor(f"SCGQ{i}", [128, 512], BF16)) for i in range(2)]
            GWS = [esM.enter_context(nc.sbuf_tensor(f"GWS{i}", [128, 512], BF16)) for i in range(2)]
            AFt = esM.enter_context(nc.sbuf_tensor("AFt", [128, NB], F32))
            AAt = esM.enter_context(nc.sbuf_tensor("AAt", [128, NB], F32))
            AKV = esM.enter_context(nc.sbuf_tensor("AKV", [128, NB], F32))
            AH = esM.enter_context(nc.sbuf_tensor("AH", [128, NB], F32))

            esMP = ExitStack()
            VP = [esMP.enter_context(nc.psum_tensor(f"VP{i}", [128, 512], F32)) for i in range(2)]
            LQP = esMP.enter_context(nc.psum_tensor("LQP", [128, 1024], F32))
            SQG = [esMP.enter_context(nc.psum_tensor(f"SQG{i}", [128, 512], F32)) for i in range(G)]

            tok_kv = {}
            tok_tg = {}
            tok_f = {}
            tok_qmul = {}
            tok_gsq = {}
            tok_gwdma = {}
            tok_ssqmm = {}
            tok_scan = {}
            tok_kvb = {}

            for cc in range(NCC):
                if cc + 2 < NCC:
                    pass
                tvb = None
                dep_a0 = [tok_wvs[cc], tok_kv.get(cc * G - 2)]
                if cc == 0:
                    dep_a0 += last_bnd_ev
                for ke in range(KE):
                    tvb = sch.op("PE", lambda e, ke=ke, cc=cc: e.matmul(
                        VP[0][:, 0:NB], WVS[cc % 2][:, ke * 128:(ke + 1) * 128],
                        XNTB[:, ke * NB:(ke + 1) * NB],
                        start=(ke == 0), stop=(ke == KE - 1)),
                        deps=(dep_a0 if ke == 0 else []), inc=(ke == KE - 1))
                tok_lfb = sch.op("PE", lambda e, cc=cc: e.matmul(
                    LQP[:, 0:NB], WK2[:, cc * 128:(cc + 1) * 128], LPTB[:],
                    start=True, stop=True),
                    deps=[tok_wk2, tok_lptb_done, tok_f.get(cc * G - 1)], inc=True)
                tok_fb = sch.op("ACT", lambda e: e.activation(AFt[:], LQP[:, 0:NB], AF.Sigmoid),
                                deps=[tok_lfb, tok_kvb.get(cc - 1)], inc=True)
                sch.op("ACT", lambda e: e.activation(AAt[:], AFt[:], AF.Sigmoid),
                       deps=[("DVE", 0)])
                tok_ab = sch.op("ACT", lambda e: e.activation(
                    AFt[:], AAt[:], AF.Copy, bias=1.0, scale=-1.0), inc=True)
                tkvb = sch.op("DVE", lambda e: e.tensor_tensor(
                    AKV[:], AFt[:], VP[0][:, 0:NB], ALU.mult), deps=[tok_ab, tvb], inc=True)
                tok_kvb[cc] = tkvb
                sch.op("DVE", lambda e: e.tensor_tensor_scan(
                    AH[:], AAt[:], AKV[:], 0.0, ALU.mult, ALU.add))
                sch.op("DVE", lambda e, cc=cc: e.tensor_copy(
                    H0C[:, cc:cc + 1], AH[:, NB - 1:NB]))
                for g in range(G):
                    i = cc * G + g
                    r = g % 2
                    if g == 0:
                        depv = [tkvb]
                    elif g == 1:
                        depv = [tok_kv.get(i - 2 - 4 + 4) or tok_kv.get((cc - 1) * G + 3)]
                        depv = [tok_kv.get((cc - 1) * G + 3)]
                    else:
                        depv = [tok_kv.get(i - 2)]
                    tok_v = None
                    for ke in range(KE):
                        tok_v = sch.op("PE", lambda e, ke=ke, cc=cc, g=g, r=r: e.matmul(
                            VP[r][:], WVS[cc % 2][:, ke * 128:(ke + 1) * 128],
                            XNT[:, ke * T + g * 512: ke * T + (g + 1) * 512],
                            start=(ke == 0), stop=(ke == KE - 1)),
                            deps=(depv if ke == 0 else []), inc=(ke == KE - 1))
                    if g == G - 1:
                        if cc + 2 < NCC:
                            tok_wvs[cc + 2] = load_wvs(cc + 2, [tok_v])
                    tok_lf = sch.op("PE", lambda e, cc=cc, g=g: e.matmul(
                        LQP[:, 0:512], WK2[:, cc * 128:(cc + 1) * 128],
                        LPT[:, g * 512:(g + 1) * 512], start=True, stop=True),
                        deps=[tok_f.get(i - 1), (tok_fb if g == 0 else None)], inc=True)
                    tok_q2 = sch.op("PE", lambda e, cc=cc, g=g: e.matmul(
                        LQP[:, 512:1024], WQ2[:, cc * 128:(cc + 1) * 128],
                        QPT[:, g * 512:(g + 1) * 512], start=True, stop=True),
                        deps=[tok_qmul.get(i - 1), tok_wq2], inc=True)
                    if i > 1:
                        gp = (i - 2) % G
                        ccp = (i - 2) // G
                        tok_ssqmm[i - 2] = sch.op("PE", lambda e, gp=gp, ccp=ccp: e.matmul(
                            SQG[gp][:], ONE[:], SCGQ[gp % 2][:],
                            start=(ccp == 0), stop=(ccp == NCC - 1)),
                            deps=[tok_gsq[i - 2], tok_one], inc=True)
                    tf = sch.op("ACT", lambda e, r=r: e.activation(
                        SCF[r][:], LQP[:, 0:512], AF.Sigmoid),
                        deps=[tok_lf, tok_kv.get(i - 2)], inc=True)
                    tok_f[i] = tf
                    sch.op("ACT", lambda e, r=r: e.activation(SCA[r][:], SCF[r][:], AF.Sigmoid),
                           deps=[tok_scan.get(i - 2)])
                    tok_a = sch.op("ACT", lambda e, r=r: e.activation(
                        SCF[r][:], SCA[r][:], AF.Copy, bias=1.0, scale=-1.0), inc=True)
                    tok_sq = sch.op("ACT", lambda e, r=r: e.activation(
                        SCSQ[r][:], LQP[:, 512:1024], AF.Sigmoid), deps=[tok_q2], inc=True)
                    tkv = sch.op("DVE", lambda e, r=r: e.tensor_tensor(
                        SCKV[r][:], SCF[r][:], VP[r][:], ALU.mult),
                        deps=[tok_a, tok_v], inc=True)
                    tok_kv[i] = tkv
                    init = H0C[:, cc:cc + 1] if g == 0 else SCH_[(g - 1) % 2][:, 511:512]
                    tok_scan[i] = sch.op(
                        "DVE", lambda e, r=r, init=init: e.tensor_tensor_scan(
                            SCH_[r][:], SCA[r][:], SCKV[r][:], init, ALU.mult, ALU.add),
                        inc=True)
                    tq = sch.op("DVE", lambda e, r=r: e.tensor_tensor(
                        SCQ[r][:], SCSQ[r][:], LQP[:, 512:1024], ALU.mult),
                        deps=[tok_sq], inc=True)
                    tok_qmul[i] = tq
                    tg = sch.op("DVE", lambda e, r=r: e.tensor_tensor(
                        SCG[r][:], SCQ[r][:], SCH_[r][:], ALU.mult),
                        deps=[tok_gsq.get(i - 2)], inc=True)
                    tok_tg[i] = tg
                    tgw = sch.op("DVE", lambda e, r=r, cc=cc: e.tensor_scalar(
                        GWS[r][:], SCG[r][:], NWC[:, cc:cc + 1], None, ALU.mult),
                        deps=[tok_gwdma.get(i - 2), tok_nwc], inc=True)
                    if i > 0:
                        rp = ((i - 1) % G) % 2
                        tok_gsq[i - 1] = sch.op("ACT", lambda e, rp=rp: e.activation(
                            SCGQ[rp][:], SCG[rp][:], AF.Square),
                            deps=[tok_tg[i - 1], tok_ssqmm.get(i - 3)], inc=True)
                    tok_gwdma[i] = sch.op("SP", lambda e, g=g, cc=cc, r=r: e.dma_start(
                        out=gw_sc.ap()[g, cc], in_=GWS[r][:]), deps=[tgw], inc=True)
            i_last = NCC * G - 1
            rl = (i_last % G) % 2
            tok_gsq[i_last] = sch.op("ACT", lambda e, rl=rl: e.activation(
                SCGQ[rl][:], SCG[rl][:], AF.Square),
                deps=[tok_tg[i_last], tok_ssqmm.get(i_last - 2)], inc=True)
            gpp = (i_last - 1) % G
            tok_ssqmm[i_last - 1] = sch.op("PE", lambda e, gpp=gpp: e.matmul(
                SQG[gpp][:], ONE[:], SCGQ[gpp % 2][:], start=False, stop=True),
                deps=[tok_gsq[i_last - 1], tok_one], inc=True)
            tok_ssq_tail = sch.op("PE", lambda e: e.matmul(
                SQG[G - 1][:], ONE[:], SCGQ[(G - 1) % 2][:], start=(NCC == 1), stop=True),
                deps=[tok_gsq[i_last], tok_one], inc=True)
        sch.barrier(real_ops={
            "PE": lambda en: en.ldweights(IDT[:]),
            "DVE": lambda en: en.tensor_copy(CONSTS[:, 3:4], CONSTS[:, 0:1]),
            "ACT": lambda en: en.activation(CONSTS[:, 4:5], CONSTS[:, 0:1], AF.Copy),
        })
        tok_ssq_done = None
        if "A" in phases:
            prev_row_dma = None
            for g in range(G):
                te = sch.op("DVE", lambda e, g=g: e.tensor_copy(SSQR[:], SQG[g][0:1, :]),
                            deps=[prev_row_dma, tok_ssq_tail], inc=True)
                prev_row_dma = sch.op("SP", lambda e, g=g: e.dma_start(
                    out=ssq_sc.ap()[g:g + 1, :], in_=SSQR[:]), deps=[te], inc=True)
                tok_ssq_done = te
            tok_ssq_dma = prev_row_dma
            tok_ssq_q = sch.queue_toks("SP")
            esMP.close()
            esM.close()
        esA.close()

        if "B" not in phases:
            sch.op("SP", lambda e: e.wait_ge(sch.sems["SP"], sch.count["SP"]))
            _replay_all(sch, block)
            return nc
        esB = ExitStack()
        WO = esB.enter_context(nc.sbuf_tensor("WO", [128, NCC * E], BF16))
        GWG = esB.enter_context(nc.sbuf_tensor("GWG", [128, NCC * 512], BF16))
        RSTD = esB.enter_context(nc.sbuf_tensor("RSTD", [128, NTS], F32))
        BST = esB.enter_context(nc.sbuf_tensor("BST", [128, 8], F32))
        ATN = esB.enter_context(nc.sbuf_tensor("ATN", [128, E], F32))
        XT = esB.enter_context(nc.sbuf_tensor("XT", [128, E], F32))
        KCH = esB.enter_context(nc.sbuf_tensor("KCH", [128, F], F32))
        KCB = esB.enter_context(nc.sbuf_tensor("KCB", [128, F], BF16))
        AMX = esB.enter_context(nc.sbuf_tensor("AMX", [128, 16], F32))
        AMP = esB.enter_context(nc.sbuf_tensor("AMP", [128, 16], F32))
        SCLF = esB.enter_context(nc.sbuf_tensor("SCLF", [128, 16], F32))
        IRP = esB.enter_context(nc.sbuf_tensor("IRP", [128, F], F32))
        SRP = esB.enter_context(nc.sbuf_tensor("SRP", [128, F], F32))
        SQJ = esB.enter_context(nc.sbuf_tensor("SQJ", [128, E // 2], BF16))

        esBP = ExitStack()
        TRP2 = [esBP.enter_context(nc.psum_tensor(f"TRP2{i}", [128, 1024], BF16)) for i in range(2)]
        OP = [esBP.enter_context(nc.psum_tensor(f"OP{i}", [128, 1024], F32)) for i in range(3)]

        tok_wo_blk = []
        if "w" in bsel:
            for cc in range(NCC):
                tok_wo_blk.append(sch.op("GP", lambda e, cc=cc: e.dma_start(
                    out=WO[:, cc * E:(cc + 1) * E],
                    in_=wo_d.rearrange("(cc p) e -> p cc e", p=128)[:, cc, :]),
                    deps=(tok_ssq_q if (cc == 0 and "A" in phases) else []), inc=True))

        tok_ssqtm = sch.op("SP", lambda e: e.dma_start(
            out=SSQTM[:], in_=ssq_sc.ap().rearrange("g (tl p) -> p (g tl)", p=128),
            allow_slow_non_contiguous=True),
            deps=[tok_ssq_dma if "A" in phases else None], inc=True)
        tok_ms = sch.op("DVE", lambda e: e.tensor_scalar(
            RSTD[:], SSQTM[:], 1.0 / C, None, ALU.mult), deps=[tok_ssqtm], inc=True)
        sch.op("ACT", lambda e: e.activation(RSTD[:], RSTD[:], AF.Ln, bias=CONSTS[:, 1:2]),
               deps=[tok_ms])
        tok_rstd = sch.op("ACT", lambda e: e.activation(
            RSTD[:], RSTD[:], AF.Exp, scale=-0.5), inc=True)

        state = dict(prev_kcb=None, prev_red=[None, None], last_tp=None,
                     amx_fin=None, irp=None, srp=None, prev_kq=None,
                     prev_quant_kcb=None, last_amax_kcb=None, ev_op2=None,
                     kch_user=None, kch_junk=None)

        def emit_kq_amax(kd):
            tl = sch.op("SP", lambda e, kd=kd: e.dma_start(
                out=KCH[:], in_=kern_d.rearrange("(kd p) f -> p kd f", p=128)[:, kd, :]),
                deps=[state["prev_kcb"]], inc=True)
            tkc = sch.op("DVE", lambda e: e.tensor_copy(KCB[:], KCH[:]),
                         deps=[tl, state["last_tp"]], inc=True)
            state["prev_kcb"] = tkc
            state["kch_user"] = tkc
            if kd == 15:
                state["last_amax_kcb"] = tkc
            tps = []
            for half in range(2):
                for j in range(8):
                    kf = half * 8 + j
                    t = sch.op("PE", lambda e, kf=kf, half=half, j=j: e.transpose(
                        TRP2[half][:, j * 128:(j + 1) * 128],
                        KCB[:, kf * 128:(kf + 1) * 128], IDT[:]),
                        deps=([tkc, state["prev_red"][half]] if j == 0 else []),
                        inc=(j == 7))
                    if j == 7:
                        tps.append(t)
            state["last_tp"] = tps[1]
            for half in range(2):
                state["prev_red"][half] = sch.op("DVE", lambda e, half=half: e.tensor_reduce(
                    AMP[:, half * 8:(half + 1) * 8],
                    TRP2[half][:].rearrange("p (j d) -> p j d", j=8),
                    AX.X, ALU.max, apply_absolute_value=True),
                    deps=[tps[half]], inc=True)
            if (alvl % 100) < 5:
                state["amx_fin"] = state["prev_red"][1]
                return
            if kd == 0:
                state["amx_fin"] = sch.op("DVE", lambda e: e.tensor_copy(AMX[:], AMP[:]),
                                          inc=True)
            else:
                state["amx_fin"] = sch.op("DVE", lambda e: e.tensor_tensor(
                    AMX[:], AMX[:], AMP[:], ALU.max), inc=True)

        def emit_kq_scales(ts):
            t1 = sch.op("DVE", lambda e: e._custom_dve(
                DIV127G, out=SCLF[:], in0=AMX[:], in1=None,
                s0=1.0 / QMAX, s1=1e-30, imm2=0.0),
                deps=[state["amx_fin"]], inc=True)
            t3 = sch.op("SP", lambda e: e.dma_start(
                out=amx_sc.ap()[1].rearrange("kf p -> p kf"), in_=SCLF[:],
                allow_slow_non_contiguous=True), deps=[t1], inc=True)
            RT = [KCB[:, 0:512], KCB[:, 512:1024]]
            tm0 = sch.op("DVE", lambda e, RT=RT: e.memset(RT[0][:], 0.0),
                         deps=[state["prev_red"][0], state["prev_red"][1], state["last_tp"]], inc=True)
            tm1 = sch.op("DVE", lambda e, RT=RT: e.memset(RT[1][:], 0.0), inc=True)
            prev_mm = {0: None, 1: None}
            prev_ev = None
            for s in range(4):
                rr = s % 2
                t4 = sch.op("GP", lambda e, s=s, rr=rr, RT=RT: e.dma_start(
                    out=RT[rr][0:1, :],
                    in_=amx_sc.ap()[1].rearrange("kf p -> (kf p)")[s * 512:(s + 1) * 512]),
                    deps=[t3, tm0, tm1, prev_mm[rr]], inc=True)
                prev_mm[rr] = sch.op("PE", lambda e, rr=rr, RT=RT: e.matmul(
                    OP[2][:, 0:512], ONE[:], RT[rr][:], start=True, stop=True),
                    deps=[t4, tok_one, tok_ev.get((ts - 1, 0)), prev_ev], inc=True)
                prev_ev = sch.op("DVE", lambda e, s=s: e.tensor_copy(
                    SRP[:, s * 512:(s + 1) * 512], OP[2][:, 0:512]),
                    deps=[prev_mm[rr]], inc=True)
            state["srp"] = prev_ev
            state["irp"] = sch.op("DVE", lambda e: e._custom_dve(
                RECIP_PS, out=IRP[:], in0=SRP[:], in1=None, **RECIP_CONSTS),
                deps=[prev_ev], inc=True)
            state["ev_op2"] = prev_ev
            state["rt_done"] = [prev_mm[0], prev_mm[1]]

        def emit_kq_quant(kd):
            dep = [state["kch_user"]] if kd > 0 else [state["last_amax_kcb"]]
            tl = sch.op("SP", lambda e, kd=kd: e.dma_start(
                out=KCH[:], in_=kern_d.rearrange("(kd p) f -> p kd f", p=128)[:, kd, :]),
                deps=dep, inc=True)
            dep2 = [tl, state["prev_kq"]] + (
                [state["last_tp"]] + state.get("rt_done", []) if kd == 0 else [])
            tkc = sch.op("DVE", lambda e: e.tensor_copy(KCB[:], KCH[:]), deps=dep2, inc=True)
            state["prev_quant_kcb"] = sch.op(
                "DVE", lambda e: e._custom_dve(
                    RNE_FQT, out=KCH[:], in0=KCB[:], in1=IRP[:],
                    s0=0.0, s1=0.0, imm2=M_ROUND),
                deps=[state["irp"]], inc=True)
            tq = sch.op("DVE", lambda e: e.tensor_tensor(KCB[:], KCH[:], SRP[:], ALU.mult),
                        deps=[state["srp"]], inc=True)
            state["kch_user"] = tq
            state["prev_kq"] = sch.op("SP", lambda e, kd=kd: e.dma_start(
                out=kq_sc.ap()[kd], in_=KCB[:]), deps=[tq], inc=True)

        tok_x2 = {}
        tok_ev = {}
        tok_tln = {}
        tok_mm_last = None
        tok_gwg = None
        prev_x2dma = None
        for ts in range(NTS):
            g, tl = ts // 4, ts % 4
            if tl == 0 and "m" in bsel:
                dep_g = [tok_mm_last] if ts > 0 else []
                tok_gwg_h = []
                for hf in range(2):
                    tok_gwg_h.append(sch.op("SP", lambda e, g=g, hf=hf: e.dma_start(
                        out=GWG[:].rearrange("p (cc t) -> p cc t", cc=NCC)[:, hf * 16:(hf + 1) * 16, :],
                        in_=gw_sc.ap()[g].rearrange("cc p t -> p cc t")[:, hf * 16:(hf + 1) * 16, :]),
                        deps=dep_g, inc=True))
            if ts < 8:
                if "a" in bsel:
                    nmax = alvl // 100 if alvl >= 100 else 16
                    if 2 * ts < nmax:
                        emit_kq_amax(2 * ts)
                    if 2 * ts + 1 < nmax:
                        emit_kq_amax(2 * ts + 1)
            elif ts == 8:
                if "s" in bsel:
                    emit_kq_scales(ts)
            else:
                if "q" in bsel:
                    for kd in range((ts - 9) * 3, min((ts - 8) * 3, 16)):
                        emit_kq_quant(kd)
            if "m" not in bsel:
                continue
            x_tok = sch.op("SP", lambda e, ts=ts: e.dma_start(
                out=XT[:], in_=x_d[ts * 128:(ts + 1) * 128, :]),
                deps=[prev_x2dma], inc=True)
            ev_toks = []
            for eh in range(2):
                rop = (2 * ts + eh) % 3
                pts, peh = divmod(2 * ts + eh - 3, 2)
                dep0 = [tok_gwg_h[0], tok_ev.get((pts, peh))]
                if ts == 0 and eh == 0:
                    dep0.append(tok_ssq_done)
                if rop == 2 and state["ev_op2"] is not None:
                    dep0.append(state["ev_op2"])
                tok_mm = None
                for cc in range(NCC):
                    dcc = (dep0 if cc == 0 else ([tok_gwg_h[1]] if cc == 16 else []))
                    if ts == 0 and eh == 0 and "w" in bsel:
                        dcc = dcc + [tok_wo_blk[cc]]
                    for sub in range(2):
                        tok_mm = sch.op("PE", lambda e, eh=eh, cc=cc, sub=sub, tl=tl, rop=rop: e.matmul(
                            OP[rop][:, sub * 512:(sub + 1) * 512],
                            GWG[:, cc * 512 + tl * 128: cc * 512 + (tl + 1) * 128],
                            WO[:, cc * E + eh * 1024 + sub * 512: cc * E + eh * 1024 + (sub + 1) * 512],
                            start=(cc == 0), stop=(cc == NCC - 1)),
                            deps=(dcc if sub == 0 else []),
                            inc=(cc == NCC - 1 and sub == 1))
                tok_mm_last = tok_mm
                tev = sch.op("DVE", lambda e, eh=eh, ts=ts, rop=rop: e.tensor_scalar(
                    ATN[:, eh * 1024:(eh + 1) * 1024], OP[rop][:],
                    RSTD[:, ts:ts + 1], None, ALU.mult),
                    deps=[tok_mm, tok_rstd, tok_tln.get(ts - 1)], inc=True)
                tok_ev[(ts, eh)] = tev
                ev_toks.append(tev)
            tsq0 = sch.op("ACT", lambda e: e.activation(
                SQJ[:], ATN[:, 0:E // 2], AF.Square, accum_out=BST[:, 5:6]),
                deps=[ev_toks[0]], inc=True)
            tsq = sch.op("ACT", lambda e: e.activation(
                SQJ[:], ATN[:, E // 2:E], AF.Square, accum_out=BST[:, 0:1]),
                deps=[ev_toks[1]], inc=True)
            sch.op("DVE", lambda e: e.tensor_reduce(
                BST[:, 1:2], ATN[:], AX.X, ALU.add, negate=True), deps=ev_toks)
            sch.op("DVE", lambda e: e.tensor_scalar(BST[:, 1:2], BST[:, 1:2], 1.0 / E, None, ALU.mult))
            sch.op("DVE", lambda e: e.tensor_tensor(BST[:, 2:3], BST[:, 1:2], BST[:, 1:2], ALU.mult))
            sch.op("DVE", lambda e: e.tensor_tensor(
                BST[:, 0:1], BST[:, 0:1], BST[:, 5:6], ALU.add), deps=[tsq0, tsq])
            sch.op("DVE", lambda e: e.tensor_scalar(BST[:, 3:4], BST[:, 0:1], 1.0 / E, None, ALU.mult))
            tvar = sch.op("DVE", lambda e: e.tensor_tensor(
                BST[:, 3:4], BST[:, 3:4], BST[:, 2:3], ALU.subtract), inc=True)
            sch.op("ACT", lambda e: e.activation(BST[:, 4:5], BST[:, 3:4], AF.Ln, bias=CONSTS[:, 0:1]),
                   deps=[tvar])
            trs = sch.op("ACT", lambda e: e.activation(
                BST[:, 4:5], BST[:, 4:5], AF.Exp, scale=-0.5), inc=True)
            tln = sch.op("DVE", lambda e: e.tensor_scalar(
                ATN[:], ATN[:], BST[:, 1:2], BST[:, 4:5], ALU.add, ALU.mult),
                deps=[trs, tsq0, tsq], inc=True)
            tok_tln[ts] = tln
            tx2 = sch.op("DVE", lambda e: e.tensor_tensor(XT[:], XT[:], ATN[:], ALU.add),
                         deps=[x_tok, tln], inc=True)
            tok_x2[ts] = tx2
            prev_x2dma = sch.op("SP", lambda e, ts=ts: e.dma_start(
                out=x2_sc.ap()[ts], in_=XT[:]), deps=[tx2], inc=True)

        esBP.close()
        esB.close()
        sch.barrier(real_ops={
            "PE": lambda en: en.ldweights(IDT[:]),
            "DVE": lambda en: en.tensor_copy(CONSTS[:, 3:4], CONSTS[:, 0:1]),
            "ACT": lambda en: en.activation(CONSTS[:, 4:5], CONSTS[:, 0:1], AF.Copy),
        })

        if "C" not in phases:
            sch.op("SP", lambda e: e.wait_ge(sch.sems["SP"], sch.count["SP"]))
            _replay_all(sch, block)
            return nc
        esC = ExitStack()
        KQ = esC.enter_context(nc.sbuf_tensor("KQ", [128, KE * F], BF16))
        X2C = [esC.enter_context(nc.sbuf_tensor(f"X2C{i}", [128, E], F32)) for i in range(3)]
        BQR = esC.enter_context(nc.sbuf_tensor("BQR", [128, F], F32))
        LNQ = esC.enter_context(nc.sbuf_tensor("LNQ", [128, E], F32))
        LNY = esC.enter_context(nc.sbuf_tensor("LNY", [128, F], F32))
        XQB = [esC.enter_context(nc.sbuf_tensor(f"XQB{i}", [128, E], BF16)) for i in range(2)]
        XQT = [esC.enter_context(nc.sbuf_tensor(f"XQT{i}", [128, E], BF16)) for i in range(2)]
        Y1 = esC.enter_context(nc.sbuf_tensor("Y1", [128, F], F32))
        CST = esC.enter_context(nc.sbuf_tensor("CST", [128, 16], F32))
        BQC = esC.enter_context(nc.sbuf_tensor("BQC", [1, 4], F32))
        BRW = esC.enter_context(nc.sbuf_tensor("BRW", [1, F], F32))
        BRB = esC.enter_context(nc.sbuf_tensor("BRB", [1, F], BF16))
        RT2 = esC.enter_context(nc.sbuf_tensor("RT2", [128, 512], BF16))

        esCP = ExitStack()
        YP = [esCP.enter_context(nc.psum_tensor(f"YP{i}", [128, 1024], F32)) for i in range(3)]
        TRP3 = [esCP.enter_context(nc.psum_tensor(f"TRP3{i}", [128, 1024], BF16)) for i in range(2)]

        tok_kq_each = []
        for kd in range(KE):
            tok_kq_each.append(sch.op("SP", lambda e, kd=kd: e.dma_start(
                out=KQ[:, kd * F:(kd + 1) * F], in_=kq_sc.ap()[kd]), inc=True))
        tok_kq_all = sch.queue_toks("SP")

        tb = sch.op("SP", lambda e: e.dma_start(out=BRW[:], in_=bias_d), inc=True)
        sch.op("DVE", lambda e: e.tensor_reduce(
            BQC[0:1, 0:1], BRW[:], AX.X, ALU.max, apply_absolute_value=True), deps=[tb])
        sch.op("DVE", lambda e: e._custom_dve(
            DIV127G, out=BQC[0:1, 2:3], in0=BQC[0:1, 0:1], in1=None,
            s0=1.0 / QMAX, s1=1e-30, imm2=0.0))
        sch.op("DVE", lambda e: e.reciprocal(BQC[0:1, 1:2], BQC[0:1, 2:3]))
        tbq = sch.op("DVE", lambda e: e._custom_dve(
            RNE_FQS, out=BRB[:], in0=BRW[:], in1=None,
            s0=BQC[0:1, 2:3], s1=BQC[0:1, 1:2], imm2=M_ROUND), inc=True)
        tm2 = sch.op("DVE", lambda e: e.memset(RT2[:], 0.0), inc=True)
        tok_bqr = None
        prev_mm2 = None
        prev_bev = None
        for s in range(4):
            tc = sch.op("DVE", lambda e, s=s: e.tensor_copy(
                RT2[0:1, :], BRB[0:1, s * 512:(s + 1) * 512]),
                deps=[tbq, tm2, prev_mm2], inc=True)
            prev_mm2 = sch.op("PE", lambda e: e.matmul(
                YP[2][:, 0:512], ONE[:], RT2[:], start=True, stop=True),
                deps=[tc, tok_one, prev_bev], inc=True)
            prev_bev = tok_bqr = sch.op("DVE", lambda e, s=s: e.tensor_copy(
                BQR[:, s * 512:(s + 1) * 512], YP[2][:, 0:512]), deps=[prev_mm2], inc=True)

        last_x2_reader = [None, None, None]
        tok_txq = {}
        tok_tps = {}
        tok_tevt = {}
        tok_ymm = {}
        tok_tadd = {}
        prev_out_dma = None
        prev_epi_ydve = None

        def emit_prep(ts):
            r3 = ts % 3
            rp = ts % 2
            tx = sch.op("SP", lambda e, ts=ts, r3=r3: e.dma_start(
                out=X2C[r3][:], in_=x2_sc.ap()[ts]), deps=[last_x2_reader[r3]], inc=True)
            tsq = sch.op("ACT", lambda e, r3=r3: e.activation(
                LNQ[:], X2C[r3][:], AF.Square, accum_out=CST[:, 0:1]),
                deps=[tx, tok_txq.get(ts - 1)], inc=True)
            sch.op("DVE", lambda e, r3=r3: e.tensor_reduce(
                CST[:, 1:2], X2C[r3][:], AX.X, ALU.add, negate=True), deps=[tx])
            sch.op("DVE", lambda e: e.tensor_scalar(CST[:, 1:2], CST[:, 1:2], 1.0 / E, None, ALU.mult))
            sch.op("DVE", lambda e: e.tensor_tensor(CST[:, 2:3], CST[:, 1:2], CST[:, 1:2], ALU.mult))
            sch.op("DVE", lambda e: e.tensor_scalar(CST[:, 3:4], CST[:, 0:1], 1.0 / E, None, ALU.mult),
                   deps=[tsq])
            tvar = sch.op("DVE", lambda e: e.tensor_tensor(
                CST[:, 3:4], CST[:, 3:4], CST[:, 2:3], ALU.subtract), inc=True)
            sch.op("ACT", lambda e: e.activation(CST[:, 4:5], CST[:, 3:4], AF.Ln, bias=CONSTS[:, 0:1]),
                   deps=[tvar])
            trs = sch.op("ACT", lambda e: e.activation(
                CST[:, 4:5], CST[:, 4:5], AF.Exp, scale=-0.5), inc=True)
            tln = sch.op("DVE", lambda e, r3=r3: e.tensor_scalar(
                LNQ[:], X2C[r3][:], CST[:, 1:2], CST[:, 4:5], ALU.add, ALU.mult),
                deps=[trs, tsq], inc=True)
            sch.op("DVE", lambda e: e.tensor_reduce(
                CST[:, 5:6], LNQ[:], AX.X, ALU.max, apply_absolute_value=True))
            sch.op("DVE", lambda e: e._custom_dve(
                DIV127G, out=CST[:, 7:8], in0=CST[:, 5:6], in1=None,
                s0=1.0 / QMAX, s1=1e-30, imm2=0.0))
            sch.op("DVE", lambda e: e.reciprocal(CST[:, 6:7], CST[:, 7:8]))
            tok_txq[ts] = sch.op("DVE", lambda e, rp=rp: e._custom_dve(
                RNE_FQS, out=XQB[rp][:], in0=LNQ[:], in1=None,
                s0=CST[:, 7:8], s1=CST[:, 6:7], imm2=M_ROUND),
                deps=[tok_tps.get(ts - 2)], inc=True)

        def emit_compute(ts):
            rp = ts % 2
            tps = []
            for half in range(2):
                for j in range(8):
                    kd = half * 8 + j
                    t = sch.op("PE", lambda e, kd=kd, half=half, j=j, rp=rp: e.transpose(
                        TRP3[half][:, j * 128:(j + 1) * 128],
                        XQB[rp][:, kd * 128:(kd + 1) * 128], IDT[:]),
                        deps=([tok_txq[ts], tok_tevt.get((ts - 1, half)), tok_idt]
                              if j == 0 else []),
                        inc=(j == 7))
                    if j == 7:
                        tps.append(t)
            tok_tps[ts] = tps[1]
            tevt = []
            for half in range(2):
                tt_ = sch.op("DVE", lambda e, half=half, rp=rp: e.tensor_copy(
                    XQT[rp][:].rearrange("p (kd t) -> p kd t", kd=KE)[:, half * 8:(half + 1) * 8, :],
                    TRP3[half][:].rearrange("p (j t) -> p j t", j=8)),
                    deps=[tps[half], tok_ymm.get(ts - 2)], inc=True)
                tok_tevt[(ts, half)] = tt_
                tevt.append(tt_)
            last_ymm = None
            for fh in range(2):
                rop = (2 * ts + fh) % 3
                dep0 = tevt + tok_kq_all + [tok_bqr, tok_tadd.get(2 * ts + fh - 3)]
                for kd in range(KE):
                    for sub in range(2):
                        last_ymm = sch.op("PE", lambda e, fh=fh, kd=kd, sub=sub, rop=rop, rp=rp: e.matmul(
                            YP[rop][:, sub * 512:(sub + 1) * 512],
                            XQT[rp][:, kd * 128:(kd + 1) * 128],
                            KQ[:, kd * F + fh * 1024 + sub * 512: kd * F + fh * 1024 + (sub + 1) * 512],
                            start=(kd == 0), stop=(kd == KE - 1)),
                            deps=(dep0 if (kd == 0 and sub == 0) else []),
                            inc=(kd == KE - 1 and sub == 1))
                tok_ymm[(ts, fh)] = last_ymm
            tok_ymm[ts] = last_ymm

        def emit_epi(ts):
            nonlocal_vars = None
            r3 = ts % 3
            res_toks = []
            for fh in range(2):
                rop = (2 * ts + fh) % 3
                tadd = sch.op("DVE", lambda e, fh=fh, rop=rop: e.tensor_tensor(
                    Y1[:, fh * 1024:(fh + 1) * 1024], YP[rop][:],
                    BQR[:, fh * 1024:(fh + 1) * 1024], ALU.add),
                    deps=[tok_ymm[(ts, fh)], tok_bqr], inc=True)
                tok_tadd[2 * ts + fh] = tadd
                res_toks.append(tadd)
            trelu = sch.op("DVE", lambda e: e.tensor_scalar_max(Y1[:], Y1[:], 0.0),
                           deps=res_toks, inc=True)
            sch.op("DVE", lambda e: e.tensor_reduce(
                CST[:, 13:14], Y1[:], AX.X, ALU.max, apply_absolute_value=True),
                deps=[trelu])
            sch.op("DVE", lambda e: e._custom_dve(
                DIV127G, out=CST[:, 15:16], in0=CST[:, 13:14], in1=None,
                s0=1.0 / QMAX, s1=1e-30, imm2=0.0))
            sch.op("DVE", lambda e: e.reciprocal(CST[:, 14:15], CST[:, 15:16]))
            tfq = sch.op("DVE", lambda e: e._custom_dve(
                RNE_FQS, out=Y1[:], in0=Y1[:], in1=None,
                s0=CST[:, 15:16], s1=CST[:, 14:15], imm2=M_ROUND),
                deps=[trelu], inc=True)
            tsq2 = sch.op("ACT", lambda e: e.activation(
                LNY[:], Y1[:], AF.Square, accum_out=CST[:, 8:9]),
                deps=[tfq, prev_out_dma], inc=True)
            sch.op("DVE", lambda e: e.tensor_reduce(
                CST[:, 9:10], Y1[:], AX.X, ALU.add, negate=True), deps=[tfq])
            sch.op("DVE", lambda e: e.tensor_scalar(CST[:, 9:10], CST[:, 9:10], 1.0 / F, None, ALU.mult))
            sch.op("DVE", lambda e: e.tensor_tensor(CST[:, 10:11], CST[:, 9:10], CST[:, 9:10], ALU.mult))
            sch.op("DVE", lambda e: e.tensor_scalar(CST[:, 11:12], CST[:, 8:9], 1.0 / F, None, ALU.mult),
                   deps=[tsq2])
            tvar2 = sch.op("DVE", lambda e: e.tensor_tensor(
                CST[:, 11:12], CST[:, 11:12], CST[:, 10:11], ALU.subtract), inc=True)
            sch.op("ACT", lambda e: e.activation(CST[:, 12:13], CST[:, 11:12], AF.Ln, bias=CONSTS[:, 0:1]),
                   deps=[tvar2])
            trs2 = sch.op("ACT", lambda e: e.activation(
                CST[:, 12:13], CST[:, 12:13], AF.Exp, scale=-0.5), inc=True)
            tln2 = sch.op("DVE", lambda e: e.tensor_scalar(
                LNY[:], Y1[:], CST[:, 9:10], CST[:, 12:13], ALU.add, ALU.mult),
                deps=[trs2, tsq2], inc=True)
            tout = sch.op("DVE", lambda e, r3=r3: e.tensor_tensor(
                LNY[:], LNY[:], X2C[r3][:], ALU.add), inc=True)
            last_x2_reader[r3] = tout
            return sch.op("SP", lambda e, ts=ts: e.dma_start(
                out=y_d[ts * 128:(ts + 1) * 128, :], in_=LNY[:]), deps=[tout], inc=True)

        for k in range(NTS + 2):
            if k < NTS:
                emit_prep(k)
            if k >= 2:
                prev_out_dma = emit_epi(k - 2)
            if 1 <= k <= NTS:
                emit_compute(k - 1)

        esCP.close()
        esC.close()

        def _final_waits(e):
            last = None
            for q in ("SP", "GP"):
                for r in range(NRING):
                    n = f"{q}{r}"
                    if sch.count[n] > 0:
                        last = e.wait_ge(sch.sems[n], sch.count[n])
            return last
        sch.ops["SP"].append(((), lambda e: _final_waits(e), None))
        _replay_all(sch, block)
    return nc


def _replay_all(sch, block):
    @block.sync
    def _(sync):
        sch.replay("SP", sync)

    @block.gpsimd
    def _(gpsimd):
        sch.replay("GP", gpsimd)

    @block.tensor
    def _(tensor):
        sch.replay("PE", tensor)

    @block.vector
    def _(vector):
        sch.replay("DVE", vector)

    @block.scalar
    def _(scalar):
        sch.replay("ACT", scalar)


_NC_CACHE = None


def kernel(L0_attn_x, L0_attn_Wq1, L0_attn_Wq2, L0_attn_Wk1, L0_attn_Wk2,
           L0_attn_Wv, L0_attn_Wo, L0_attn_norm_w, L0_ffn_kernel, L0_ffn_bias):
    global _NC_CACHE
    x = np.ascontiguousarray(np.asarray(L0_attn_x, dtype=np.float32).reshape(NCORES * T, E))
    wq1 = np.ascontiguousarray(np.asarray(L0_attn_Wq1, dtype=np.float32))
    wq2 = np.ascontiguousarray(np.asarray(L0_attn_Wq2, dtype=np.float32))
    wk1 = np.ascontiguousarray(np.asarray(L0_attn_Wk1, dtype=np.float32))
    wk2 = np.ascontiguousarray(np.asarray(L0_attn_Wk2, dtype=np.float32))
    wv = np.ascontiguousarray(np.asarray(L0_attn_Wv, dtype=np.float32))
    wo = np.ascontiguousarray(np.asarray(L0_attn_Wo, dtype=np.float32))
    nw = np.ascontiguousarray(np.asarray(L0_attn_norm_w, dtype=np.float32).reshape(NCC, 128))
    kern = np.ascontiguousarray(np.asarray(L0_ffn_kernel, dtype=np.float32))
    bias = np.ascontiguousarray(np.asarray(L0_ffn_bias, dtype=np.float32).reshape(1, F))
    ident = np.eye(128, dtype=ml_dtypes.bfloat16)
    ones = np.ones((128, 128), dtype=ml_dtypes.bfloat16)

    if _NC_CACHE is None:
        _NC_CACHE = build_program()
    nc = _NC_CACHE

    in_maps = []
    for c in range(NCORES):
        xs = np.ascontiguousarray(x[c * T:(c + 1) * T])
        if c % 2 == 1:
            xbs = np.ascontiguousarray(x[c * T - NB: c * T])
        else:
            xbs = np.zeros((NB, E), np.float32)
        in_maps.append(dict(x=xs, xb=xbs, wq1=wq1, wk1=wk1, wq2=wq2, wk2=wk2,
                            wv=wv, wo=wo, kern=kern, biasr=bias, nwr=nw,
                            identb=ident, onesb=ones))
    res = run_bass_kernel_spmd(nc, in_maps, core_ids=list(range(NCORES)))
    out = np.concatenate([res.results[c]["y"] for c in range(NCORES)], axis=0)
    return out.reshape(4, 4096, E).astype(np.float32)


# version 09 (final)
# speedup vs baseline: 1.0114x; 1.0072x over previous
import numpy as np
import ml_dtypes
from contextlib import ExitStack

import concourse.bass as bass
import concourse.mybir as mybir
from concourse.bass_utils import run_bass_kernel_spmd


def _register_fq_ops():
    from concourse import dve_ops as D
    from concourse.dve_op import DveOp
    from concourse.dve_spec import (Spec, Src0, Src1, C0, C1, C2, lower,
                                    maxx, AluOp, Bin)

    have = {o.name: o for o in D.OPS}
    if "RNE_FQS" in have:
        return tuple(have[n] for n in ("DIV127G", "RNE_FQS", "RECIP_PS", "RNE_FQT"))

    donors = iter(("DVEHASH32L_SEED", "DVEHASH32L_BODY", "DVEHASH32L_DRAIN",
                   "TENSOR_MASK_FP8"))

    def _mk(name, body, ref):
        spec = Spec(body=body)
        sha = lower(spec, ver="v3", perf_en=False, name=name).sha("v3")
        op = DveOp(name, spec, subdim=False, uops_sha={"v3": sha}, reference=ref)
        D.OPS.append(op)
        D.CUSTOM_DVE_SPECS[name] = spec
        D.CUSTOM_DVE_REFS[name] = op.ref
        D._SUB_OPCODE_FOR_NAME[name] = D._SUB_OPCODE_FOR_NAME[next(donors)]
        return op

    def _f64(a):
        return a.dtype == np.float64

    div127 = _mk("DIV127G", maxx(Src0, C1) * C0,
                 lambda in0, in1, s0, s1, imm2:
                 np.maximum(in0, s1) / 127.0 if _f64(in0)
                 else np.maximum(in0, np.float32(s1)) * np.float32(s0))
    fqs = _mk("RNE_FQS", ((Src0 * C1 + C2) - C2) * C0,
              lambda in0, in1, s0, s1, imm2:
              np.round(in0 * (1.0 / s0)) * s0 if _f64(in0)
              else np.round(in0 * s1) * s0)
    notx = Bin(AluOp.BITWISE_NOT, Src0, Src0)
    y0 = notx * C0
    y1 = y0 * (C1 - Src0 * y0)

    def _ref_recip_pass(in0, in1, s0, s1, imm2):
        if _f64(in0):
            return in0.copy()
        not_x = (~in0.view(np.int32)).view(np.float32)
        y0 = not_x * np.float32(s0)
        y1 = y0 * (np.float32(s1) - in0 * y0)
        return y1 * (np.float32(imm2) - in0 * y1)

    rp = _mk("RECIP_PS", y1 * (C2 - Src0 * y1), _ref_recip_pass)
    fqt = _mk("RNE_FQT", (Src0 * Src1 + C2) - C2,
              lambda in0, in1, s0, s1, imm2:
              np.round(in0 * (1.0 / in1)) if _f64(in0)
              else np.round(in0 * in1))
    return div127, fqs, rp, fqt


RECIP_CONSTS = dict(s0=-0.23549792, s1=2.0017324, imm2=2.0)
DIV127G, RNE_FQS, RECIP_PS, RNE_FQT = _register_fq_ops()

F32 = mybir.dt.float32
BF16 = mybir.dt.bfloat16
AF = mybir.ActivationFunctionType
ALU = mybir.AluOpType
AX = mybir.AxisListType

NCORES = 8
T = 2048
NTS = 16
G = 4
E = 2048
KE = 16
C = 4096
NCC = 32
HD = 128
F = 2048
NB = 256
M_ROUND = 12582912.0
QMAX = 127.0


def _flt(deps):
    return tuple(d for d in deps if d is not None)


NRING = 8


class Sched:

    ENGINES = ("SP", "GP", "PE", "DVE", "ACT")

    def __init__(self, nc):
        self.nc = nc
        self.ops = {e: [] for e in self.ENGINES}
        self.count = {}
        self.dma_i = {"SP": 0, "GP": 0}
        self.sems = {}
        self.semnames = []
        for e in ("PE", "DVE", "ACT", "PL"):
            self.semnames.append(e)
            self.count[e] = 0
        for q in ("SP", "GP"):
            for r in range(NRING):
                n = f"{q}{r}"
                self.semnames.append(n)
                self.count[n] = 0

    def alloc_sems(self, es):
        for n in self.semnames:
            self.sems[n] = es.enter_context(self.nc.semaphore(f"s_{n}"))

    def op(self, eng, fn, deps=(), inc=False):
        deps = list(_flt(deps))
        if eng == "PLC":
            self.count["PL"] += 1
            tok = ("PL", self.count["PL"])
            self.ops["GP"].append((tuple(deps), fn, tok))
            return tok
        if eng in ("DVE", "ACT"):
            inc = True
        tok = None
        if eng in ("SP", "GP"):
            i = self.dma_i[eng]
            self.dma_i[eng] = i + 1
            sem = f"{eng}{i % NRING}"
            if self.count[sem] > 0:
                deps.append((sem, self.count[sem]))
            self.count[sem] += 16
            tok = (sem, self.count[sem])
        elif inc:
            self.count[eng] += 1
            tok = (eng, self.count[eng])
        self.ops[eng].append((tuple(deps), fn, tok))
        return tok

    def queue_toks(self, q):
        return [(f"{q}{r}", self.count[f"{q}{r}"]) for r in range(NRING)]

    def barrier(self, real_ops):
        toks = [("PL", self.count["PL"])]
        for e in ("PE", "DVE", "ACT"):
            toks.append(self.op(e, real_ops[e], inc=True))
        toks += self.queue_toks("SP") + self.queue_toks("GP")
        for e in self.ENGINES:
            self.ops[e].append((tuple(toks), None, None))
        return toks

    def replay(self, eng_name, engine):
        waited = {}
        self_serial = eng_name in ("DVE", "ACT")
        prev_count = 0
        prev_plc = 0
        for deps, fn, tok in self.ops[eng_name]:
            for (sname, val) in deps:
                if sname == eng_name or val <= 0:
                    continue
                if waited.get(sname, 0) < val:
                    engine.wait_ge(self.sems[sname], val)
                    waited[sname] = val
            if fn is None:
                continue
            is_plc = tok is not None and tok[0] == "PL"
            if self_serial and prev_count > 0:
                engine.wait_ge(self.sems[eng_name], prev_count)
            if is_plc and prev_plc > 0 and waited.get("PL", 0) < prev_plc:
                engine.wait_ge(self.sems["PL"], prev_plc)
                waited["PL"] = max(waited.get("PL", 0), prev_plc)
            inst = fn(engine)
            if tok is not None:
                step = 16 if tok[0] not in ("PE", "DVE", "ACT", "PL") else 1
                inst.then_inc(self.sems[tok[0]], step)
                if tok[0] == eng_name:
                    prev_count = tok[1]
                if is_plc:
                    prev_plc = tok[1]


def build_program(phases="ABC", bsel="wasqm", alvl=9):
    nc = _build_program(phases, bsel, alvl)
    mybir.codegen_inst_isa_subclasses(nc)
    return nc


def _build_program(phases="ABC", bsel="wasqm", alvl=9):
    _alvl_n = alvl
    nc = bass.Bass("TRN2", target_bir_lowering=False)

    x_d = nc.dram_tensor("x", [T, E], F32, kind="ExternalInput").ap()
    xb_d = nc.dram_tensor("xb", [NB, E], F32, kind="ExternalInput").ap()
    wq1_d = nc.dram_tensor("wq1", [E, HD], F32, kind="ExternalInput").ap()
    wk1_d = nc.dram_tensor("wk1", [E, HD], F32, kind="ExternalInput").ap()
    wq2_d = nc.dram_tensor("wq2", [HD, C], F32, kind="ExternalInput").ap()
    wk2_d = nc.dram_tensor("wk2", [HD, C], F32, kind="ExternalInput").ap()
    wv_d = nc.dram_tensor("wv", [E, C], F32, kind="ExternalInput").ap()
    wo_d = nc.dram_tensor("wo", [C, E], F32, kind="ExternalInput").ap()
    kern_d = nc.dram_tensor("kern", [E, F], F32, kind="ExternalInput").ap()
    bias_d = nc.dram_tensor("biasr", [1, F], F32, kind="ExternalInput").ap()
    nw_d = nc.dram_tensor("nwr", [NCC, 128], F32, kind="ExternalInput").ap()
    identb_d = nc.dram_tensor("identb", [128, 128], BF16, kind="ExternalInput").ap()
    onesb_d = nc.dram_tensor("onesb", [128, 128], BF16, kind="ExternalInput").ap()
    y_d = nc.dram_tensor("y", [T, E], F32, kind="ExternalOutput").ap()

    gw_sc = nc.dram_tensor("gw_sc", [G, NCC, 128, 512], BF16)
    x2_sc = nc.dram_tensor("x2_sc", [NTS, 128, E], F32)
    kq_sc = nc.dram_tensor("kq_sc", [KE, 128, F], BF16)
    ssq_sc = nc.dram_tensor("ssq_sc", [G, 512], F32)
    amx_sc = nc.dram_tensor("amx_sc", [2, 16, 128], F32)

    sch = Sched(nc)
    es = ExitStack()
    sch.alloc_sems(es)

    IDT = es.enter_context(nc.sbuf_tensor("IDT", [128, 128], BF16))
    ONE = es.enter_context(nc.sbuf_tensor("ONE", [128, 128], BF16))
    SSQTM = es.enter_context(nc.sbuf_tensor("SSQTM", [128, NTS], F32))
    CONSTS = es.enter_context(nc.sbuf_tensor("CONSTS", [128, 6], F32))

    with nc.Block() as block, es:
        esA = ExitStack()
        XNT = esA.enter_context(nc.sbuf_tensor("XNT", [128, KE * T], BF16))
        XNTB = esA.enter_context(nc.sbuf_tensor("XNTB", [128, KE * NB], BF16))
        WVS = [esA.enter_context(nc.sbuf_tensor(f"WVS{i}", [128, KE * 128], BF16)) for i in range(2)]
        WK2 = esA.enter_context(nc.sbuf_tensor("WK2", [128, C], BF16))
        WQ2 = esA.enter_context(nc.sbuf_tensor("WQ2", [128, C], BF16))
        QPT = esA.enter_context(nc.sbuf_tensor("QPT", [128, T], BF16))
        LPT = esA.enter_context(nc.sbuf_tensor("LPT", [128, T], BF16))
        LPTB = esA.enter_context(nc.sbuf_tensor("LPTB", [128, NB], BF16))
        NWC = esA.enter_context(nc.sbuf_tensor("NWC", [128, NCC], F32))
        H0C = esA.enter_context(nc.sbuf_tensor("H0C", [128, NCC], F32))
        SSQR = esA.enter_context(nc.sbuf_tensor("SSQR", [1, 512], F32))

        sch.op("DVE", lambda e: e.memset(CONSTS[:, 0:1], 1e-5))
        sch.op("DVE", lambda e: e.memset(CONSTS[:, 1:2], 1e-6))
        tok_consts = sch.op("DVE", lambda e: e.memset(CONSTS[:, 2:3], 1.0), inc=True)
        tok_idt = sch.op("SP", lambda e: e.dma_start(out=IDT[:], in_=identb_d), inc=True)
        tok_one = sch.op("SP", lambda e: e.dma_start(out=ONE[:], in_=onesb_d), inc=True)
        tok_nwc = sch.op("SP", lambda e: e.dma_start(
            out=NWC[:], in_=nw_d.rearrange("cc p -> p cc"),
            allow_slow_non_contiguous=True), inc=True)
        tok_wk2 = sch.op("GP", lambda e: e.dma_start(out=WK2[:], in_=wk2_d), inc=True)
        tok_wq2 = sch.op("GP", lambda e: e.dma_start(out=WQ2[:], in_=wq2_d), inc=True)

        if "A" in phases:
            esP = ExitStack()
            XP = [esP.enter_context(nc.sbuf_tensor(f"XP{i}", [128, E], F32)) for i in range(2)]
            WQ1 = esP.enter_context(nc.sbuf_tensor("WQ1", [128, KE * HD], BF16))
            WK1 = esP.enter_context(nc.sbuf_tensor("WK1", [128, KE * HD], BF16))
            XNB = [esP.enter_context(nc.sbuf_tensor(f"XNB{i}", [128, E], BF16)) for i in range(2)]
            PSS = esP.enter_context(nc.sbuf_tensor("PSS", [128, NTS + 2], F32))
            NMS = esP.enter_context(nc.sbuf_tensor("NMS", [128, NTS + 2], F32))
            RSS = esP.enter_context(nc.sbuf_tensor("RSS", [128, NTS + 2], F32))
            TV1 = esP.enter_context(nc.sbuf_tensor("TV1", [128, 1], F32))
            TV2 = esP.enter_context(nc.sbuf_tensor("TV2", [128, 1], F32))

            tok_wq1 = sch.op("GP", lambda e: e.dma_start(
                out=WQ1[:].rearrange("p (ke h) -> p ke h", ke=KE),
                in_=wq1_d.rearrange("(ke p) h -> p ke h", p=128)), inc=True)
            tok_wk1 = sch.op("GP", lambda e: e.dma_start(
                out=WK1[:].rearrange("p (ke h) -> p ke h", ke=KE),
                in_=wk1_d.rearrange("(ke p) h -> p ke h", p=128)), inc=True)

            def load_wvs(cc, deps):
                return sch.op("GP", lambda e, cc=cc: e.dma_start(
                    out=WVS[cc % 2][:].rearrange("p (ke q) -> p ke q", ke=KE),
                    in_=wv_d.rearrange("(ke p) c -> p ke c", p=128)[:, :, cc * 128:(cc + 1) * 128]),
                    deps=deps, inc=True)

            tok_wvs = {}
            tok_wvs[0] = load_wvs(0, [])
            tok_wvs[1] = load_wvs(1, [])

            esPP = ExitStack()
            TRP = [esPP.enter_context(nc.psum_tensor(f"TRP{i}", [128, 1024], BF16)) for i in range(2)]
            QPP = esPP.enter_context(nc.psum_tensor("QPP", [128, 128], F32))
            LPP = esPP.enter_context(nc.psum_tensor("LPP", [128, 128], F32))

            last_xp_reader = [None, None]
            prev_ev = [None, None]
            prev_tp_last = [None, None]
            prev_qev = None
            prev_lev = None
            prev_ln_tok = None
            last_bnd_ev = []
            tok_lptb_done = None
            stat_toks = {}

            def p_stats(ts):
                nonlocal prev_ln_tok
                r = ts % 2
                bnd = ts >= NTS
                src_ = (x_d[ts * 128:(ts + 1) * 128, :] if not bnd
                        else xb_d[(ts - NTS) * 128:(ts - NTS + 1) * 128, :])
                tok_x = sch.op("SP", lambda e, src_=src_, r=r: e.dma_start(
                    out=XP[r][:], in_=src_), deps=[last_xp_reader[r]], inc=True)
                tok_sq = sch.op("ACT", lambda e, r=r, ts=ts: e.activation(
                    XNB[r][:], XP[r][:], AF.Square, accum_out=PSS[:, ts:ts + 1]),
                    deps=[tok_x, prev_tp_last[r]], inc=True)
                sch.op("DVE", lambda e, r=r, ts=ts: e.tensor_reduce(
                    NMS[:, ts:ts + 1], XP[r][:], AX.X, ALU.add, negate=True),
                    deps=[tok_x, prev_ln_tok])
                sch.op("DVE", lambda e, ts=ts: e.tensor_scalar(
                    NMS[:, ts:ts + 1], NMS[:, ts:ts + 1], 1.0 / E, None, ALU.mult))
                sch.op("DVE", lambda e, ts=ts: e.tensor_tensor(
                    TV1[:], NMS[:, ts:ts + 1], NMS[:, ts:ts + 1], ALU.mult))
                sch.op("DVE", lambda e, ts=ts: e.tensor_scalar(
                    TV2[:], PSS[:, ts:ts + 1], 1.0 / E, None, ALU.mult), deps=[tok_sq])
                tok_v2 = sch.op("DVE", lambda e: e.tensor_tensor(
                    TV2[:], TV2[:], TV1[:], ALU.subtract), inc=True)
                prev_ln_tok = sch.op("ACT", lambda e, ts=ts: e.activation(
                    RSS[:, ts:ts + 1], TV2[:], AF.Ln, bias=CONSTS[:, 0:1]),
                    deps=[tok_v2, tok_consts], inc=True)
                tok_rs = sch.op("ACT", lambda e, ts=ts: e.activation(
                    RSS[:, ts:ts + 1], RSS[:, ts:ts + 1], AF.Exp, scale=-0.5), inc=True)
                stat_toks[ts] = (tok_sq, tok_rs)

            def p_finish(ts):
                nonlocal prev_qev, prev_lev, tok_lptb_done
                r = ts % 2
                bnd = ts >= NTS
                tok_sq, tok_rs = stat_toks[ts]
                tok_xn = sch.op("DVE", lambda e, r=r, ts=ts: e.tensor_scalar(
                    XNB[r][:], XP[r][:], NMS[:, ts:ts + 1], RSS[:, ts:ts + 1],
                    ALU.add, ALU.mult), deps=[tok_sq, tok_rs], inc=True)
                last_xp_reader[r] = tok_xn
                tok_tp = [None, None]
                for half in range(2):
                    for j in range(8):
                        ke = half * 8 + j
                        t = sch.op("PE", lambda e, ke=ke, half=half, j=j, r=r: e.transpose(
                            TRP[half][:, j * 128:(j + 1) * 128],
                            XNB[r][:, ke * 128:(ke + 1) * 128], IDT[:]),
                            deps=([tok_xn, tok_idt, prev_ev[half]] if j == 0 else []),
                            inc=(j == 7))
                        if j == 7:
                            tok_tp[half] = t
                prev_tp_last[r] = tok_tp[1]
                evs = []
                for half in range(2):
                    if not bnd:
                        dst = XNT[:].rearrange("p (ke t) -> p ke t", ke=KE)[
                            :, half * 8:(half + 1) * 8, ts * 128:(ts + 1) * 128]
                    else:
                        dst = XNTB[:].rearrange("p (ke t) -> p ke t", ke=KE)[
                            :, half * 8:(half + 1) * 8, (ts - NTS) * 128:(ts - NTS + 1) * 128]
                    ev = sch.op("DVE", lambda e, dst=dst, half=half: e.tensor_copy(
                        dst, TRP[half][:].rearrange("p (j t) -> p j t", j=8)),
                        deps=[tok_tp[half]], inc=True)
                    prev_ev[half] = ev
                    evs.append(ev)
                if bnd:
                    last_bnd_ev.extend(evs)
                if not bnd:
                    tq_mm = None
                    for ke in range(KE):
                        tq_mm = sch.op("PE", lambda e, ke=ke, ts=ts: e.matmul(
                            QPP[:], WQ1[:, ke * 128:(ke + 1) * 128],
                            XNT[:, ke * T + ts * 128: ke * T + (ts + 1) * 128],
                            start=(ke == 0), stop=(ke == KE - 1)),
                            deps=(evs + [tok_wq1, prev_qev] if ke == 0 else []),
                            inc=(ke == KE - 1))
                    tl_mm = None
                    for ke in range(KE):
                        tl_mm = sch.op("PE", lambda e, ke=ke, ts=ts: e.matmul(
                            LPP[:], WK1[:, ke * 128:(ke + 1) * 128],
                            XNT[:, ke * T + ts * 128: ke * T + (ts + 1) * 128],
                            start=(ke == 0), stop=(ke == KE - 1)),
                            deps=([tok_wk1, prev_lev] if ke == 0 else []),
                            inc=(ke == KE - 1))
                    prev_qev = sch.op("DVE", lambda e, ts=ts: e.tensor_copy(
                        QPT[:, ts * 128:(ts + 1) * 128], QPP[:]), deps=[tq_mm], inc=True)
                    prev_lev = sch.op("DVE", lambda e, ts=ts: e.tensor_copy(
                        LPT[:, ts * 128:(ts + 1) * 128], LPP[:]), deps=[tl_mm], inc=True)
                else:
                    tb_ = ts - NTS
                    tl_mm = None
                    for ke in range(KE):
                        tl_mm = sch.op("PE", lambda e, ke=ke, tb_=tb_: e.matmul(
                            LPP[:], WK1[:, ke * 128:(ke + 1) * 128],
                            XNTB[:, ke * NB + tb_ * 128: ke * NB + (tb_ + 1) * 128],
                            start=(ke == 0), stop=(ke == KE - 1)),
                            deps=(evs + [prev_lev] if ke == 0 else []),
                            inc=(ke == KE - 1))
                    prev_lev = tok_lptb_done = sch.op("DVE", lambda e, tb_=tb_: e.tensor_copy(
                        LPTB[:, tb_ * 128:(tb_ + 1) * 128], LPP[:]), deps=[tl_mm], inc=True)

            for k in range(NTS + 3):
                if k < NTS + 2:
                    p_stats(k)
                if k >= 1:
                    p_finish(k - 1)

            esPP.close()
            esP.close()

            esM = ExitStack()
            SCF = [esM.enter_context(nc.sbuf_tensor(f"SCF{i}", [128, 512], F32)) for i in range(2)]
            SCA = [esM.enter_context(nc.sbuf_tensor(f"SCA{i}", [128, 512], F32)) for i in range(2)]
            SCKV = [esM.enter_context(nc.sbuf_tensor(f"SCKV{i}", [128, 512], F32)) for i in range(2)]
            SCH_ = [esM.enter_context(nc.sbuf_tensor(f"SCH{i}", [128, 512], F32)) for i in range(2)]
            SCSQ = [esM.enter_context(nc.sbuf_tensor(f"SCSQ{i}", [128, 512], F32)) for i in range(2)]
            SCQ = [esM.enter_context(nc.sbuf_tensor(f"SCQ{i}", [128, 512], F32)) for i in range(2)]
            SCG = [esM.enter_context(nc.sbuf_tensor(f"SCG{i}", [128, 512], F32)) for i in range(2)]
            SCGQ = [esM.enter_context(nc.sbuf_tensor(f"SCGQ{i}", [128, 512], BF16)) for i in range(2)]
            GWS = [esM.enter_context(nc.sbuf_tensor(f"GWS{i}", [128, 512], BF16)) for i in range(2)]
            AFt = esM.enter_context(nc.sbuf_tensor("AFt", [128, NB], F32))
            AAt = esM.enter_context(nc.sbuf_tensor("AAt", [128, NB], F32))
            AKV = esM.enter_context(nc.sbuf_tensor("AKV", [128, NB], F32))
            AH = esM.enter_context(nc.sbuf_tensor("AH", [128, NB], F32))

            esMP = ExitStack()
            VP = [esMP.enter_context(nc.psum_tensor(f"VP{i}", [128, 512], F32)) for i in range(2)]
            LQP = esMP.enter_context(nc.psum_tensor("LQP", [128, 1024], F32))
            SQG = [esMP.enter_context(nc.psum_tensor(f"SQG{i}", [128, 512], F32)) for i in range(G)]

            tok_kv = {}
            tok_tg = {}
            tok_f = {}
            tok_qmul = {}
            tok_gsq = {}
            tok_gwdma = {}
            tok_ssqmm = {}
            tok_scan = {}
            tok_kvb = {}

            for cc in range(NCC):
                if cc + 2 < NCC:
                    pass
                tvb = None
                dep_a0 = [tok_wvs[cc], tok_kv.get(cc * G - 2)]
                if cc == 0:
                    dep_a0 += last_bnd_ev
                for ke in range(KE):
                    tvb = sch.op("PE", lambda e, ke=ke, cc=cc: e.matmul(
                        VP[0][:, 0:NB], WVS[cc % 2][:, ke * 128:(ke + 1) * 128],
                        XNTB[:, ke * NB:(ke + 1) * NB],
                        start=(ke == 0), stop=(ke == KE - 1)),
                        deps=(dep_a0 if ke == 0 else []), inc=(ke == KE - 1))
                tok_lfb = sch.op("PE", lambda e, cc=cc: e.matmul(
                    LQP[:, 0:NB], WK2[:, cc * 128:(cc + 1) * 128], LPTB[:],
                    start=True, stop=True),
                    deps=[tok_wk2, tok_lptb_done, tok_f.get(cc * G - 1)], inc=True)
                tok_fb = sch.op("ACT", lambda e: e.activation(AFt[:], LQP[:, 0:NB], AF.Sigmoid),
                                deps=[tok_lfb, tok_kvb.get(cc - 1)], inc=True)
                sch.op("ACT", lambda e: e.activation(AAt[:], AFt[:], AF.Sigmoid),
                       deps=[("DVE", 0)])
                tok_ab = sch.op("ACT", lambda e: e.activation(
                    AFt[:], AAt[:], AF.Copy, bias=1.0, scale=-1.0), inc=True)
                tkvb = sch.op("DVE", lambda e: e.tensor_tensor(
                    AKV[:], AFt[:], VP[0][:, 0:NB], ALU.mult), deps=[tok_ab, tvb], inc=True)
                tok_kvb[cc] = tkvb
                sch.op("DVE", lambda e: e.tensor_tensor_scan(
                    AH[:], AAt[:], AKV[:], 0.0, ALU.mult, ALU.add))
                sch.op("DVE", lambda e, cc=cc: e.tensor_copy(
                    H0C[:, cc:cc + 1], AH[:, NB - 1:NB]))
                for g in range(G):
                    i = cc * G + g
                    r = g % 2
                    if g == 0:
                        depv = [tkvb]
                    elif g == 1:
                        depv = [tok_kv.get(i - 2 - 4 + 4) or tok_kv.get((cc - 1) * G + 3)]
                        depv = [tok_kv.get((cc - 1) * G + 3)]
                    else:
                        depv = [tok_kv.get(i - 2)]
                    tok_v = None
                    for ke in range(KE):
                        tok_v = sch.op("PE", lambda e, ke=ke, cc=cc, g=g, r=r: e.matmul(
                            VP[r][:], WVS[cc % 2][:, ke * 128:(ke + 1) * 128],
                            XNT[:, ke * T + g * 512: ke * T + (g + 1) * 512],
                            start=(ke == 0), stop=(ke == KE - 1)),
                            deps=(depv if ke == 0 else []), inc=(ke == KE - 1))
                    if g == G - 1:
                        if cc + 2 < NCC:
                            tok_wvs[cc + 2] = load_wvs(cc + 2, [tok_v])
                    tok_lf = sch.op("PE", lambda e, cc=cc, g=g: e.matmul(
                        LQP[:, 0:512], WK2[:, cc * 128:(cc + 1) * 128],
                        LPT[:, g * 512:(g + 1) * 512], start=True, stop=True),
                        deps=[tok_f.get(i - 1), (tok_fb if g == 0 else None)], inc=True)
                    tok_q2 = sch.op("PE", lambda e, cc=cc, g=g: e.matmul(
                        LQP[:, 512:1024], WQ2[:, cc * 128:(cc + 1) * 128],
                        QPT[:, g * 512:(g + 1) * 512], start=True, stop=True),
                        deps=[tok_qmul.get(i - 1), tok_wq2], inc=True)
                    if i > 1:
                        gp = (i - 2) % G
                        ccp = (i - 2) // G
                        tok_ssqmm[i - 2] = sch.op("PE", lambda e, gp=gp, ccp=ccp: e.matmul(
                            SQG[gp][:], ONE[:], SCGQ[gp % 2][:],
                            start=(ccp == 0), stop=(ccp == NCC - 1)),
                            deps=[tok_gsq[i - 2], tok_one], inc=True)
                    tf = sch.op("ACT", lambda e, r=r: e.activation(
                        SCF[r][:], LQP[:, 0:512], AF.Sigmoid),
                        deps=[tok_lf, tok_kv.get(i - 2)], inc=True)
                    tok_f[i] = tf
                    sch.op("ACT", lambda e, r=r: e.activation(SCA[r][:], SCF[r][:], AF.Sigmoid),
                           deps=[tok_scan.get(i - 2)])
                    tok_a = sch.op("ACT", lambda e, r=r: e.activation(
                        SCF[r][:], SCA[r][:], AF.Copy, bias=1.0, scale=-1.0), inc=True)
                    tok_sq = sch.op("ACT", lambda e, r=r: e.activation(
                        SCSQ[r][:], LQP[:, 512:1024], AF.Sigmoid), deps=[tok_q2], inc=True)
                    tkv = sch.op("DVE", lambda e, r=r: e.tensor_tensor(
                        SCKV[r][:], SCF[r][:], VP[r][:], ALU.mult),
                        deps=[tok_a, tok_v], inc=True)
                    tok_kv[i] = tkv
                    init = H0C[:, cc:cc + 1] if g == 0 else SCH_[(g - 1) % 2][:, 511:512]
                    tok_scan[i] = sch.op(
                        "DVE", lambda e, r=r, init=init: e.tensor_tensor_scan(
                            SCH_[r][:], SCA[r][:], SCKV[r][:], init, ALU.mult, ALU.add),
                        inc=True)
                    tq = sch.op("DVE", lambda e, r=r: e.tensor_tensor(
                        SCQ[r][:], SCSQ[r][:], LQP[:, 512:1024], ALU.mult),
                        deps=[tok_sq], inc=True)
                    tok_qmul[i] = tq
                    tg = sch.op("DVE", lambda e, r=r: e.tensor_tensor(
                        SCG[r][:], SCQ[r][:], SCH_[r][:], ALU.mult),
                        deps=[tok_gsq.get(i - 2)], inc=True)
                    tok_tg[i] = tg
                    tgw = sch.op("DVE", lambda e, r=r, cc=cc: e.tensor_scalar(
                        GWS[r][:], SCG[r][:], NWC[:, cc:cc + 1], None, ALU.mult),
                        deps=[tok_gwdma.get(i - 2), tok_nwc], inc=True)
                    if i > 0:
                        rp = ((i - 1) % G) % 2
                        tok_gsq[i - 1] = sch.op("ACT", lambda e, rp=rp: e.activation(
                            SCGQ[rp][:], SCG[rp][:], AF.Square),
                            deps=[tok_tg[i - 1], tok_ssqmm.get(i - 3)], inc=True)
                    tok_gwdma[i] = sch.op("SP", lambda e, g=g, cc=cc, r=r: e.dma_start(
                        out=gw_sc.ap()[g, cc], in_=GWS[r][:]), deps=[tgw], inc=True)
            i_last = NCC * G - 1
            rl = (i_last % G) % 2
            tok_gsq[i_last] = sch.op("ACT", lambda e, rl=rl: e.activation(
                SCGQ[rl][:], SCG[rl][:], AF.Square),
                deps=[tok_tg[i_last], tok_ssqmm.get(i_last - 2)], inc=True)
            gpp = (i_last - 1) % G
            tok_ssqmm[i_last - 1] = sch.op("PE", lambda e, gpp=gpp: e.matmul(
                SQG[gpp][:], ONE[:], SCGQ[gpp % 2][:], start=False, stop=True),
                deps=[tok_gsq[i_last - 1], tok_one], inc=True)
            tok_ssq_tail = sch.op("PE", lambda e: e.matmul(
                SQG[G - 1][:], ONE[:], SCGQ[(G - 1) % 2][:], start=(NCC == 1), stop=True),
                deps=[tok_gsq[i_last], tok_one], inc=True)
        sch.barrier(real_ops={
            "PE": lambda en: en.ldweights(IDT[:]),
            "DVE": lambda en: en.tensor_copy(CONSTS[:, 3:4], CONSTS[:, 0:1]),
            "ACT": lambda en: en.activation(CONSTS[:, 4:5], CONSTS[:, 0:1], AF.Copy),
        })
        tok_ssq_done = None
        if "A" in phases:
            prev_row_dma = None
            for g in range(G):
                te = sch.op("DVE", lambda e, g=g: e.tensor_copy(SSQR[:], SQG[g][0:1, :]),
                            deps=[prev_row_dma, tok_ssq_tail], inc=True)
                prev_row_dma = sch.op("SP", lambda e, g=g: e.dma_start(
                    out=ssq_sc.ap()[g:g + 1, :], in_=SSQR[:]), deps=[te], inc=True)
                tok_ssq_done = te
            tok_ssq_dma = prev_row_dma
            tok_ssq_q = sch.queue_toks("SP")
            esMP.close()
            esM.close()
        esA.close()

        if "B" not in phases:
            sch.op("SP", lambda e: e.wait_ge(sch.sems["SP"], sch.count["SP"]))
            _replay_all(sch, block)
            return nc
        esB = ExitStack()
        WO = esB.enter_context(nc.sbuf_tensor("WO", [128, NCC * E], BF16))
        GWG = esB.enter_context(nc.sbuf_tensor("GWG", [128, NCC * 512], BF16))
        RSTD = esB.enter_context(nc.sbuf_tensor("RSTD", [128, NTS], F32))
        BST = esB.enter_context(nc.sbuf_tensor("BST", [128, 8], F32))
        ATN = esB.enter_context(nc.sbuf_tensor("ATN", [128, E], F32))
        XT = esB.enter_context(nc.sbuf_tensor("XT", [128, E], F32))
        KCH = esB.enter_context(nc.sbuf_tensor("KCH", [128, F], F32))
        KCB = esB.enter_context(nc.sbuf_tensor("KCB", [128, F], BF16))
        AMX = esB.enter_context(nc.sbuf_tensor("AMX", [128, 16], F32))
        AMP = esB.enter_context(nc.sbuf_tensor("AMP", [128, 16], F32))
        SCLF = esB.enter_context(nc.sbuf_tensor("SCLF", [128, 16], F32))
        IRP = esB.enter_context(nc.sbuf_tensor("IRP", [128, F], F32))
        SRP = esB.enter_context(nc.sbuf_tensor("SRP", [128, F], F32))
        SQJ = esB.enter_context(nc.sbuf_tensor("SQJ", [128, E // 2], BF16))

        esBP = ExitStack()
        TRP2 = [esBP.enter_context(nc.psum_tensor(f"TRP2{i}", [128, 1024], BF16)) for i in range(2)]
        OP = [esBP.enter_context(nc.psum_tensor(f"OP{i}", [128, 1024], F32)) for i in range(3)]

        tok_wo_blk = []
        if "w" in bsel:
            for cc in range(NCC):
                tok_wo_blk.append(sch.op("GP", lambda e, cc=cc: e.dma_start(
                    out=WO[:, cc * E:(cc + 1) * E],
                    in_=wo_d.rearrange("(cc p) e -> p cc e", p=128)[:, cc, :]),
                    deps=(tok_ssq_q if (cc == 0 and "A" in phases) else []), inc=True))

        tok_ssqtm = sch.op("SP", lambda e: e.dma_start(
            out=SSQTM[:], in_=ssq_sc.ap().rearrange("g (tl p) -> p (g tl)", p=128),
            allow_slow_non_contiguous=True),
            deps=[tok_ssq_dma if "A" in phases else None], inc=True)
        tok_ms = sch.op("DVE", lambda e: e.tensor_scalar(
            RSTD[:], SSQTM[:], 1.0 / C, None, ALU.mult), deps=[tok_ssqtm], inc=True)
        sch.op("ACT", lambda e: e.activation(RSTD[:], RSTD[:], AF.Ln, bias=CONSTS[:, 1:2]),
               deps=[tok_ms])
        tok_rstd = sch.op("ACT", lambda e: e.activation(
            RSTD[:], RSTD[:], AF.Exp, scale=-0.5), inc=True)

        state = dict(prev_kcb=None, prev_red=[None, None], last_tp=None,
                     amx_fin=None, irp=None, srp=None, prev_kq=None,
                     prev_quant_kcb=None, last_amax_kcb=None, ev_op2=None,
                     kch_user=None, kch_junk=None)

        def emit_kq_amax(kd):
            tl = sch.op("SP", lambda e, kd=kd: e.dma_start(
                out=KCH[:], in_=kern_d.rearrange("(kd p) f -> p kd f", p=128)[:, kd, :]),
                deps=[state["prev_kcb"]], inc=True)
            tkc = sch.op("DVE", lambda e: e.tensor_copy(KCB[:], KCH[:]),
                         deps=[tl, state["last_tp"]], inc=True)
            state["prev_kcb"] = tkc
            state["kch_user"] = tkc
            if kd == 15:
                state["last_amax_kcb"] = tkc
            tps = []
            for half in range(2):
                for j in range(8):
                    kf = half * 8 + j
                    t = sch.op("PE", lambda e, kf=kf, half=half, j=j: e.transpose(
                        TRP2[half][:, j * 128:(j + 1) * 128],
                        KCB[:, kf * 128:(kf + 1) * 128], IDT[:]),
                        deps=([tkc, state["prev_red"][half]] if j == 0 else []),
                        inc=(j == 7))
                    if j == 7:
                        tps.append(t)
            state["last_tp"] = tps[1]
            for half in range(2):
                state["prev_red"][half] = sch.op("DVE", lambda e, half=half: e.tensor_reduce(
                    AMP[:, half * 8:(half + 1) * 8],
                    TRP2[half][:].rearrange("p (j d) -> p j d", j=8),
                    AX.X, ALU.max, apply_absolute_value=True),
                    deps=[tps[half]], inc=True)
            if (alvl % 100) < 5:
                state["amx_fin"] = state["prev_red"][1]
                return
            if kd == 0:
                state["amx_fin"] = sch.op("DVE", lambda e: e.tensor_copy(AMX[:], AMP[:]),
                                          inc=True)
            else:
                state["amx_fin"] = sch.op("DVE", lambda e: e.tensor_tensor(
                    AMX[:], AMX[:], AMP[:], ALU.max), inc=True)

        def emit_kq_scales(ts):
            t1 = sch.op("DVE", lambda e: e._custom_dve(
                DIV127G, out=SCLF[:], in0=AMX[:], in1=None,
                s0=1.0 / QMAX, s1=1e-30, imm2=0.0),
                deps=[state["amx_fin"]], inc=True)
            t3 = sch.op("SP", lambda e: e.dma_start(
                out=amx_sc.ap()[1].rearrange("kf p -> p kf"), in_=SCLF[:],
                allow_slow_non_contiguous=True), deps=[t1], inc=True)
            RT = [KCB[:, 0:512], KCB[:, 512:1024]]
            tm0 = sch.op("DVE", lambda e, RT=RT: e.memset(RT[0][:], 0.0),
                         deps=[state["prev_red"][0], state["prev_red"][1], state["last_tp"]], inc=True)
            tm1 = sch.op("DVE", lambda e, RT=RT: e.memset(RT[1][:], 0.0), inc=True)
            prev_mm = {0: None, 1: None}
            prev_ev = None
            for s in range(4):
                rr = s % 2
                t4 = sch.op("GP", lambda e, s=s, rr=rr, RT=RT: e.dma_start(
                    out=RT[rr][0:1, :],
                    in_=amx_sc.ap()[1].rearrange("kf p -> (kf p)")[s * 512:(s + 1) * 512]),
                    deps=[t3, tm0, tm1, prev_mm[rr]], inc=True)
                prev_mm[rr] = sch.op("PE", lambda e, rr=rr, RT=RT: e.matmul(
                    OP[2][:, 0:512], ONE[:], RT[rr][:], start=True, stop=True),
                    deps=[t4, tok_one, tok_ev.get((ts - 1, 0)), prev_ev], inc=True)
                prev_ev = sch.op("DVE", lambda e, s=s: e.tensor_copy(
                    SRP[:, s * 512:(s + 1) * 512], OP[2][:, 0:512]),
                    deps=[prev_mm[rr]], inc=True)
            state["srp"] = prev_ev
            state["irp"] = sch.op("DVE", lambda e: e._custom_dve(
                RECIP_PS, out=IRP[:], in0=SRP[:], in1=None, **RECIP_CONSTS),
                deps=[prev_ev], inc=True)
            state["ev_op2"] = prev_ev
            state["rt_done"] = [prev_mm[0], prev_mm[1]]

        def emit_kq_quant(kd):
            dep = [state["kch_user"]] if kd > 0 else [state["last_amax_kcb"]]
            tl = sch.op("SP", lambda e, kd=kd: e.dma_start(
                out=KCH[:], in_=kern_d.rearrange("(kd p) f -> p kd f", p=128)[:, kd, :]),
                deps=dep, inc=True)
            dep2 = [tl, state["prev_kq"]] + (
                [state["last_tp"]] + state.get("rt_done", []) if kd == 0 else [])
            tkc = sch.op("DVE", lambda e: e.tensor_copy(KCB[:], KCH[:]), deps=dep2, inc=True)
            state["prev_quant_kcb"] = sch.op(
                "DVE", lambda e: e._custom_dve(
                    RNE_FQT, out=KCH[:], in0=KCB[:], in1=IRP[:],
                    s0=0.0, s1=0.0, imm2=M_ROUND),
                deps=[state["irp"]], inc=True)
            tq = sch.op("DVE", lambda e: e.tensor_tensor(KCB[:], KCH[:], SRP[:], ALU.mult),
                        deps=[state["srp"]], inc=True)
            state["kch_user"] = tq
            state["prev_kq"] = sch.op("SP", lambda e, kd=kd: e.dma_start(
                out=kq_sc.ap()[kd], in_=KCB[:]), deps=[tq], inc=True)

        tok_x2 = {}
        tok_ev = {}
        tok_tln = {}
        tok_mm_last = None
        tok_gwg = None
        prev_x2dma = None
        for ts in range(NTS):
            g, tl = ts // 4, ts % 4
            if tl == 0 and "m" in bsel:
                dep_g = [tok_mm_last] if ts > 0 else []
                tok_gwg_h = []
                for hf in range(2):
                    tok_gwg_h.append(sch.op("SP", lambda e, g=g, hf=hf: e.dma_start(
                        out=GWG[:].rearrange("p (cc t) -> p cc t", cc=NCC)[:, hf * 16:(hf + 1) * 16, :],
                        in_=gw_sc.ap()[g].rearrange("cc p t -> p cc t")[:, hf * 16:(hf + 1) * 16, :]),
                        deps=dep_g, inc=True))
            if ts < 8:
                if "a" in bsel:
                    nmax = alvl // 100 if alvl >= 100 else 16
                    if 2 * ts < nmax:
                        emit_kq_amax(2 * ts)
                    if 2 * ts + 1 < nmax:
                        emit_kq_amax(2 * ts + 1)
            elif ts == 8:
                if "s" in bsel:
                    emit_kq_scales(ts)
            else:
                if "q" in bsel:
                    for kd in range((ts - 9) * 3, min((ts - 8) * 3, 16)):
                        emit_kq_quant(kd)
            if "m" not in bsel:
                continue
            x_tok = sch.op("SP", lambda e, ts=ts: e.dma_start(
                out=XT[:], in_=x_d[ts * 128:(ts + 1) * 128, :]),
                deps=[prev_x2dma], inc=True)
            ev_toks = []
            for eh in range(2):
                rop = (2 * ts + eh) % 3
                pts, peh = divmod(2 * ts + eh - 3, 2)
                dep0 = [tok_gwg_h[0], tok_ev.get((pts, peh))]
                if ts == 0 and eh == 0:
                    dep0.append(tok_ssq_done)
                if rop == 2 and state["ev_op2"] is not None:
                    dep0.append(state["ev_op2"])
                tok_mm = None
                for cc in range(NCC):
                    dcc = (dep0 if cc == 0 else ([tok_gwg_h[1]] if cc == 16 else []))
                    if ts == 0 and eh == 0 and "w" in bsel:
                        dcc = dcc + [tok_wo_blk[cc]]
                    for sub in range(2):
                        tok_mm = sch.op("PE", lambda e, eh=eh, cc=cc, sub=sub, tl=tl, rop=rop: e.matmul(
                            OP[rop][:, sub * 512:(sub + 1) * 512],
                            GWG[:, cc * 512 + tl * 128: cc * 512 + (tl + 1) * 128],
                            WO[:, cc * E + eh * 1024 + sub * 512: cc * E + eh * 1024 + (sub + 1) * 512],
                            start=(cc == 0), stop=(cc == NCC - 1)),
                            deps=(dcc if sub == 0 else []),
                            inc=(cc == NCC - 1 and sub == 1))
                tok_mm_last = tok_mm
                tev = sch.op("DVE", lambda e, eh=eh, ts=ts, rop=rop: e.tensor_scalar(
                    ATN[:, eh * 1024:(eh + 1) * 1024], OP[rop][:],
                    RSTD[:, ts:ts + 1], None, ALU.mult),
                    deps=[tok_mm, tok_rstd, tok_tln.get(ts - 1)], inc=True)
                tok_ev[(ts, eh)] = tev
                ev_toks.append(tev)
            tsq0 = sch.op("ACT", lambda e: e.activation(
                SQJ[:], ATN[:, 0:E // 2], AF.Square, accum_out=BST[:, 5:6]),
                deps=[ev_toks[0]], inc=True)
            tsq = sch.op("ACT", lambda e: e.activation(
                SQJ[:], ATN[:, E // 2:E], AF.Square, accum_out=BST[:, 0:1]),
                deps=[ev_toks[1]], inc=True)
            sch.op("DVE", lambda e: e.tensor_reduce(
                BST[:, 1:2], ATN[:], AX.X, ALU.add, negate=True), deps=ev_toks)
            sch.op("DVE", lambda e: e.tensor_scalar(BST[:, 1:2], BST[:, 1:2], 1.0 / E, None, ALU.mult))
            sch.op("DVE", lambda e: e.tensor_tensor(BST[:, 2:3], BST[:, 1:2], BST[:, 1:2], ALU.mult))
            sch.op("DVE", lambda e: e.tensor_tensor(
                BST[:, 0:1], BST[:, 0:1], BST[:, 5:6], ALU.add), deps=[tsq0, tsq])
            sch.op("DVE", lambda e: e.tensor_scalar(BST[:, 3:4], BST[:, 0:1], 1.0 / E, None, ALU.mult))
            tvar = sch.op("DVE", lambda e: e.tensor_tensor(
                BST[:, 3:4], BST[:, 3:4], BST[:, 2:3], ALU.subtract), inc=True)
            sch.op("ACT", lambda e: e.activation(BST[:, 4:5], BST[:, 3:4], AF.Ln, bias=CONSTS[:, 0:1]),
                   deps=[tvar])
            trs = sch.op("ACT", lambda e: e.activation(
                BST[:, 4:5], BST[:, 4:5], AF.Exp, scale=-0.5), inc=True)
            tln = sch.op("DVE", lambda e: e.tensor_scalar(
                ATN[:], ATN[:], BST[:, 1:2], BST[:, 4:5], ALU.add, ALU.mult),
                deps=[trs, tsq0, tsq], inc=True)
            tok_tln[ts] = tln
            tx2 = sch.op("DVE", lambda e: e.tensor_tensor(XT[:], XT[:], ATN[:], ALU.add),
                         deps=[x_tok, tln], inc=True)
            tok_x2[ts] = tx2
            prev_x2dma = sch.op("SP", lambda e, ts=ts: e.dma_start(
                out=x2_sc.ap()[ts], in_=XT[:]), deps=[tx2], inc=True)

        esBP.close()
        esB.close()
        sch.barrier(real_ops={
            "PE": lambda en: en.ldweights(IDT[:]),
            "DVE": lambda en: en.tensor_copy(CONSTS[:, 3:4], CONSTS[:, 0:1]),
            "ACT": lambda en: en.activation(CONSTS[:, 4:5], CONSTS[:, 0:1], AF.Copy),
        })

        if "C" not in phases:
            sch.op("SP", lambda e: e.wait_ge(sch.sems["SP"], sch.count["SP"]))
            _replay_all(sch, block)
            return nc
        esC = ExitStack()
        KQ = esC.enter_context(nc.sbuf_tensor("KQ", [128, KE * F], BF16))
        X2C = [esC.enter_context(nc.sbuf_tensor(f"X2C{i}", [128, E], F32)) for i in range(3)]
        BQR = esC.enter_context(nc.sbuf_tensor("BQR", [128, F], F32))
        LNQ = esC.enter_context(nc.sbuf_tensor("LNQ", [128, E], F32))
        LNY = esC.enter_context(nc.sbuf_tensor("LNY", [128, F], F32))
        XQB = [esC.enter_context(nc.sbuf_tensor(f"XQB{i}", [128, E], BF16)) for i in range(2)]
        XQT = [esC.enter_context(nc.sbuf_tensor(f"XQT{i}", [128, E], BF16)) for i in range(2)]
        Y1 = esC.enter_context(nc.sbuf_tensor("Y1", [128, F], F32))
        CST = esC.enter_context(nc.sbuf_tensor("CST", [128, 16], F32))
        BQC = esC.enter_context(nc.sbuf_tensor("BQC", [1, 4], F32))
        BRW = esC.enter_context(nc.sbuf_tensor("BRW", [1, F], F32))
        BRB = esC.enter_context(nc.sbuf_tensor("BRB", [1, F], BF16))
        RT2 = esC.enter_context(nc.sbuf_tensor("RT2", [128, 512], BF16))

        esCP = ExitStack()
        YP = [esCP.enter_context(nc.psum_tensor(f"YP{i}", [128, 1024], F32)) for i in range(3)]
        TRP3 = [esCP.enter_context(nc.psum_tensor(f"TRP3{i}", [128, 1024], BF16)) for i in range(2)]

        tok_kq_each = []
        for kd in range(KE):
            tok_kq_each.append(sch.op("SP", lambda e, kd=kd: e.dma_start(
                out=KQ[:, kd * F:(kd + 1) * F], in_=kq_sc.ap()[kd]), inc=True))
        tok_kq_all = sch.queue_toks("SP")

        tb = sch.op("SP", lambda e: e.dma_start(out=BRW[:], in_=bias_d), inc=True)
        sch.op("DVE", lambda e: e.tensor_reduce(
            BQC[0:1, 0:1], BRW[:], AX.X, ALU.max, apply_absolute_value=True), deps=[tb])
        sch.op("DVE", lambda e: e._custom_dve(
            DIV127G, out=BQC[0:1, 2:3], in0=BQC[0:1, 0:1], in1=None,
            s0=1.0 / QMAX, s1=1e-30, imm2=0.0))
        sch.op("DVE", lambda e: e.reciprocal(BQC[0:1, 1:2], BQC[0:1, 2:3]))
        tbq = sch.op("DVE", lambda e: e._custom_dve(
            RNE_FQS, out=BRB[:], in0=BRW[:], in1=None,
            s0=BQC[0:1, 2:3], s1=BQC[0:1, 1:2], imm2=M_ROUND), inc=True)
        tm2 = sch.op("DVE", lambda e: e.memset(RT2[:], 0.0), inc=True)
        tok_bqr = None
        prev_mm2 = None
        prev_bev = None
        for s in range(4):
            tc = sch.op("DVE", lambda e, s=s: e.tensor_copy(
                RT2[0:1, :], BRB[0:1, s * 512:(s + 1) * 512]),
                deps=[tbq, tm2, prev_mm2], inc=True)
            prev_mm2 = sch.op("PE", lambda e: e.matmul(
                YP[2][:, 0:512], ONE[:], RT2[:], start=True, stop=True),
                deps=[tc, tok_one, prev_bev], inc=True)
            prev_bev = tok_bqr = sch.op("DVE", lambda e, s=s: e.tensor_copy(
                BQR[:, s * 512:(s + 1) * 512], YP[2][:, 0:512]), deps=[prev_mm2], inc=True)

        last_x2_reader = [None, None, None]
        tok_txq = {}
        tok_tps = {}
        tok_tevt = {}
        tok_ymm = {}
        tok_tadd = {}
        prev_out_dma = None
        prev_epi_ydve = None

        def emit_prep(ts):
            r3 = ts % 3
            rp = ts % 2
            tx = sch.op("SP", lambda e, ts=ts, r3=r3: e.dma_start(
                out=X2C[r3][:], in_=x2_sc.ap()[ts]), deps=[last_x2_reader[r3]], inc=True)
            tsq = sch.op("ACT", lambda e, r3=r3: e.activation(
                LNQ[:], X2C[r3][:], AF.Square, accum_out=CST[:, 0:1]),
                deps=[tx, tok_txq.get(ts - 1)], inc=True)
            sch.op("DVE", lambda e, r3=r3: e.tensor_reduce(
                CST[:, 1:2], X2C[r3][:], AX.X, ALU.add, negate=True), deps=[tx])
            sch.op("DVE", lambda e: e.tensor_scalar(CST[:, 1:2], CST[:, 1:2], 1.0 / E, None, ALU.mult))
            sch.op("DVE", lambda e: e.tensor_tensor(CST[:, 2:3], CST[:, 1:2], CST[:, 1:2], ALU.mult))
            sch.op("DVE", lambda e: e.tensor_scalar(CST[:, 3:4], CST[:, 0:1], 1.0 / E, None, ALU.mult),
                   deps=[tsq])
            tvar = sch.op("DVE", lambda e: e.tensor_tensor(
                CST[:, 3:4], CST[:, 3:4], CST[:, 2:3], ALU.subtract), inc=True)
            sch.op("ACT", lambda e: e.activation(CST[:, 4:5], CST[:, 3:4], AF.Ln, bias=CONSTS[:, 0:1]),
                   deps=[tvar])
            trs = sch.op("ACT", lambda e: e.activation(
                CST[:, 4:5], CST[:, 4:5], AF.Exp, scale=-0.5), inc=True)
            tln = sch.op("DVE", lambda e, r3=r3: e.tensor_scalar(
                LNQ[:], X2C[r3][:], CST[:, 1:2], CST[:, 4:5], ALU.add, ALU.mult),
                deps=[trs, tsq], inc=True)
            sch.op("DVE", lambda e: e.tensor_reduce(
                CST[:, 5:6], LNQ[:], AX.X, ALU.max, apply_absolute_value=True))
            sch.op("DVE", lambda e: e._custom_dve(
                DIV127G, out=CST[:, 7:8], in0=CST[:, 5:6], in1=None,
                s0=1.0 / QMAX, s1=1e-30, imm2=0.0))
            sch.op("DVE", lambda e: e.reciprocal(CST[:, 6:7], CST[:, 7:8]))
            tok_txq[ts] = sch.op("DVE", lambda e, rp=rp: e._custom_dve(
                RNE_FQS, out=XQB[rp][:], in0=LNQ[:], in1=None,
                s0=CST[:, 7:8], s1=CST[:, 6:7], imm2=M_ROUND),
                deps=[tok_tps.get(ts - 2)], inc=True)

        def emit_tr(ts):
            rp = ts % 2
            tps = []
            for half in range(2):
                for j in range(8):
                    kd = half * 8 + j
                    t = sch.op("PE", lambda e, kd=kd, half=half, j=j, rp=rp: e.transpose(
                        TRP3[half][:, j * 128:(j + 1) * 128],
                        XQB[rp][:, kd * 128:(kd + 1) * 128], IDT[:]),
                        deps=([tok_txq[ts], tok_tevt.get((ts - 1, half)), tok_idt]
                              if j == 0 else []),
                        inc=(j == 7))
                    if j == 7:
                        tps.append(t)
            tok_tps[ts] = tps[1]
            tevt = []
            for half in range(2):
                tt_ = sch.op("DVE", lambda e, half=half, rp=rp: e.tensor_copy(
                    XQT[rp][:].rearrange("p (kd t) -> p kd t", kd=KE)[:, half * 8:(half + 1) * 8, :],
                    TRP3[half][:].rearrange("p (j t) -> p j t", j=8)),
                    deps=[tps[half], tok_ymm.get(ts - 2)], inc=True)
                tok_tevt[(ts, half)] = tt_
                tevt.append(tt_)
            return tevt

        def emit_compute(ts, tevt):
            rp = ts % 2
            last_ymm = None
            for fh in range(2):
                rop = (2 * ts + fh) % 3
                dep0 = tevt + tok_kq_all + [tok_bqr, tok_tadd.get(2 * ts + fh - 3)]
                for kd in range(KE):
                    for sub in range(2):
                        last_ymm = sch.op("PE", lambda e, fh=fh, kd=kd, sub=sub, rop=rop, rp=rp: e.matmul(
                            YP[rop][:, sub * 512:(sub + 1) * 512],
                            XQT[rp][:, kd * 128:(kd + 1) * 128],
                            KQ[:, kd * F + fh * 1024 + sub * 512: kd * F + fh * 1024 + (sub + 1) * 512],
                            start=(kd == 0), stop=(kd == KE - 1)),
                            deps=(dep0 if (kd == 0 and sub == 0) else []),
                            inc=(kd == KE - 1 and sub == 1))
                tok_ymm[(ts, fh)] = last_ymm
            tok_ymm[ts] = last_ymm

        def emit_epi(ts):
            nonlocal_vars = None
            r3 = ts % 3
            res_toks = []
            for fh in range(2):
                rop = (2 * ts + fh) % 3
                tadd = sch.op("DVE", lambda e, fh=fh, rop=rop: e.tensor_tensor(
                    Y1[:, fh * 1024:(fh + 1) * 1024], YP[rop][:],
                    BQR[:, fh * 1024:(fh + 1) * 1024], ALU.add),
                    deps=[tok_ymm[(ts, fh)], tok_bqr], inc=True)
                tok_tadd[2 * ts + fh] = tadd
                res_toks.append(tadd)
            trelu = sch.op("DVE", lambda e: e.tensor_scalar_max(Y1[:], Y1[:], 0.0),
                           deps=res_toks, inc=True)
            sch.op("DVE", lambda e: e.tensor_reduce(
                CST[:, 13:14], Y1[:], AX.X, ALU.max, apply_absolute_value=True),
                deps=[trelu])
            sch.op("DVE", lambda e: e._custom_dve(
                DIV127G, out=CST[:, 15:16], in0=CST[:, 13:14], in1=None,
                s0=1.0 / QMAX, s1=1e-30, imm2=0.0))
            sch.op("DVE", lambda e: e.reciprocal(CST[:, 14:15], CST[:, 15:16]))
            tfq = sch.op("DVE", lambda e: e._custom_dve(
                RNE_FQS, out=Y1[:], in0=Y1[:], in1=None,
                s0=CST[:, 15:16], s1=CST[:, 14:15], imm2=M_ROUND),
                deps=[trelu], inc=True)
            tsq2 = sch.op("ACT", lambda e: e.activation(
                LNY[:], Y1[:], AF.Square, accum_out=CST[:, 8:9]),
                deps=[tfq, prev_out_dma], inc=True)
            sch.op("DVE", lambda e: e.tensor_reduce(
                CST[:, 9:10], Y1[:], AX.X, ALU.add, negate=True), deps=[tfq])
            sch.op("DVE", lambda e: e.tensor_scalar(CST[:, 9:10], CST[:, 9:10], 1.0 / F, None, ALU.mult))
            sch.op("DVE", lambda e: e.tensor_tensor(CST[:, 10:11], CST[:, 9:10], CST[:, 9:10], ALU.mult))
            sch.op("DVE", lambda e: e.tensor_scalar(CST[:, 11:12], CST[:, 8:9], 1.0 / F, None, ALU.mult),
                   deps=[tsq2])
            tvar2 = sch.op("DVE", lambda e: e.tensor_tensor(
                CST[:, 11:12], CST[:, 11:12], CST[:, 10:11], ALU.subtract), inc=True)
            sch.op("ACT", lambda e: e.activation(CST[:, 12:13], CST[:, 11:12], AF.Ln, bias=CONSTS[:, 0:1]),
                   deps=[tvar2])
            trs2 = sch.op("ACT", lambda e: e.activation(
                CST[:, 12:13], CST[:, 12:13], AF.Exp, scale=-0.5), inc=True)
            tln2 = sch.op("DVE", lambda e: e.tensor_scalar(
                LNY[:], Y1[:], CST[:, 9:10], CST[:, 12:13], ALU.add, ALU.mult),
                deps=[trs2, tsq2], inc=True)
            tout = sch.op("DVE", lambda e, r3=r3: e.tensor_tensor(
                LNY[:], LNY[:], X2C[r3][:], ALU.add), inc=True)
            last_x2_reader[r3] = tout
            return sch.op("SP", lambda e, ts=ts: e.dma_start(
                out=y_d[ts * 128:(ts + 1) * 128, :], in_=LNY[:]), deps=[tout], inc=True)

        tevt_d = {}
        for k in range(NTS + 2):
            if k < NTS:
                emit_prep(k)
            if 1 <= k <= NTS:
                tevt_d[k - 1] = emit_tr(k - 1)
            if k >= 2:
                prev_out_dma = emit_epi(k - 2)
            if 1 <= k <= NTS:
                emit_compute(k - 1, tevt_d.pop(k - 1))

        esCP.close()
        esC.close()

        def _final_waits(e):
            last = None
            for q in ("SP", "GP"):
                for r in range(NRING):
                    n = f"{q}{r}"
                    if sch.count[n] > 0:
                        last = e.wait_ge(sch.sems[n], sch.count[n])
            return last
        sch.ops["SP"].append(((), lambda e: _final_waits(e), None))
        _replay_all(sch, block)
    return nc


def _replay_all(sch, block):
    @block.sync
    def _(sync):
        sch.replay("SP", sync)

    @block.gpsimd
    def _(gpsimd):
        sch.replay("GP", gpsimd)

    @block.tensor
    def _(tensor):
        sch.replay("PE", tensor)

    @block.vector
    def _(vector):
        sch.replay("DVE", vector)

    @block.scalar
    def _(scalar):
        sch.replay("ACT", scalar)


_NC_CACHE = None


def kernel(L0_attn_x, L0_attn_Wq1, L0_attn_Wq2, L0_attn_Wk1, L0_attn_Wk2,
           L0_attn_Wv, L0_attn_Wo, L0_attn_norm_w, L0_ffn_kernel, L0_ffn_bias):
    global _NC_CACHE
    x = np.ascontiguousarray(np.asarray(L0_attn_x, dtype=np.float32).reshape(NCORES * T, E))
    wq1 = np.ascontiguousarray(np.asarray(L0_attn_Wq1, dtype=np.float32))
    wq2 = np.ascontiguousarray(np.asarray(L0_attn_Wq2, dtype=np.float32))
    wk1 = np.ascontiguousarray(np.asarray(L0_attn_Wk1, dtype=np.float32))
    wk2 = np.ascontiguousarray(np.asarray(L0_attn_Wk2, dtype=np.float32))
    wv = np.ascontiguousarray(np.asarray(L0_attn_Wv, dtype=np.float32))
    wo = np.ascontiguousarray(np.asarray(L0_attn_Wo, dtype=np.float32))
    nw = np.ascontiguousarray(np.asarray(L0_attn_norm_w, dtype=np.float32).reshape(NCC, 128))
    kern = np.ascontiguousarray(np.asarray(L0_ffn_kernel, dtype=np.float32))
    bias = np.ascontiguousarray(np.asarray(L0_ffn_bias, dtype=np.float32).reshape(1, F))
    ident = np.eye(128, dtype=ml_dtypes.bfloat16)
    ones = np.ones((128, 128), dtype=ml_dtypes.bfloat16)

    if _NC_CACHE is None:
        _NC_CACHE = build_program()
    nc = _NC_CACHE

    in_maps = []
    for c in range(NCORES):
        xs = np.ascontiguousarray(x[c * T:(c + 1) * T])
        if c % 2 == 1:
            xbs = np.ascontiguousarray(x[c * T - NB: c * T])
        else:
            xbs = np.zeros((NB, E), np.float32)
        in_maps.append(dict(x=xs, xb=xbs, wq1=wq1, wk1=wk1, wq2=wq2, wk2=wk2,
                            wv=wv, wo=wo, kern=kern, biasr=bias, nwr=nw,
                            identb=ident, onesb=ones))
    res = run_bass_kernel_spmd(nc, in_maps, core_ids=list(range(NCORES)))
    out = np.concatenate([res.results[c]["y"] for c in range(NCORES)], axis=0)
    return out.reshape(4, 4096, E).astype(np.float32)
